# version 15
# baseline (speedup 1.0000x reference)
import math
from contextlib import ExitStack

import numpy as np

import concourse.bass as bass
import concourse.mybir as mybir
from concourse.bass_utils import run_bass_kernel_spmd

F32 = mybir.dt.float32
BF16 = mybir.dt.bfloat16
AF = mybir.ActivationFunctionType
ALU = mybir.AluOpType
AX = mybir.AxisListType

DM = 4096
TPC = 1024
NCH = 32
NG = 8
INVSQD = 1.0 / math.sqrt(128.0)
EPS = 1e-5
LN_N = float(DM)
TWO_PI = 6.283185307179586
INV_2PI = 1.0 / TWO_PI
CW_C1 = 6.28125
CW_C2 = TWO_PI - CW_C1
MAGIC = 12582912.0
HALF_PI = math.pi / 2.0

ENGINES = ("sync", "scalar", "vector", "tensor", "gpsimd")

XA0 = 0
BST0 = 2048
AT0 = 10240
WS0 = 10240
WS1 = 14336
QS0 = 8192
VT0 = 18432
MK0 = 0
QB0 = 8192
PR0 = 12288
PSM0 = 18432
QDB0 = 19456

_CACHE = {}
_FLAGS = set()


class _StopEmit(Exception):
    pass


_PHASES = ("p0", "ln1", "xa", "f1", "ln2", "k", "v", "cc", "q", "p3", "p4",
           "ln3", "f2")


def build(upto="f2"):
    f1_noc = NCH
    if upto.startswith("f1."):
        f1_noc = int(upto.split(".")[1])
        upto = "f1"
    stop_at = _PHASES.index(upto)

    def on(ph):
        return _PHASES.index(ph) <= stop_at

    nc = bass.Bass("TRN2", target_bir_lowering=False, debug=False)

    xin = nc.dram_tensor("xin", [DM, TPC], F32, kind="ExternalInput")
    wq = nc.dram_tensor("wq", [DM, DM], F32, kind="ExternalInput")
    wk = nc.dram_tensor("wk", [1024, DM], F32, kind="ExternalInput")
    wv = nc.dram_tensor("wv", [1024, DM], F32, kind="ExternalInput")
    wo = nc.dram_tensor("wo", [DM, DM], F32, kind="ExternalInput")
    wft = nc.dram_tensor("wft", [DM, DM], F32, kind="ExternalInput")
    bq = nc.dram_tensor("bq", [DM], F32, kind="ExternalInput")
    bk = nc.dram_tensor("bk", [1024], F32, kind="ExternalInput")
    bv = nc.dram_tensor("bv", [1024], F32, kind="ExternalInput")
    bo_ = nc.dram_tensor("bo", [DM], F32, kind="ExternalInput")
    bf_ = nc.dram_tensor("bf", [DM], F32, kind="ExternalInput")
    atm = nc.dram_tensor("atm", [DM, 256], F32, kind="ExternalInput")
    bstm = nc.dram_tensor("bstm", [256, DM], F32, kind="ExternalInput")
    cfD = nc.dram_tensor("cf", [16], F32, kind="ExternalInput")
    scD = nc.dram_tensor("sc", [16], F32, kind="ExternalInput")
    posD = nc.dram_tensor("posr", [1, TPC], F32, kind="ExternalInput")
    outD = nc.dram_tensor("out", [DM, TPC], F32, kind="ExternalOutput")

    xr1 = nc.dram_tensor("xr1", [DM, TPC], F32)
    xr2 = nc.dram_tensor("xr2", [DM, TPC], F32)
    qd = nc.dram_tensor("qdr", [128, 32 * TPC], BF16)
    kcon = nc.dram_tensor("kcon", [1024, TPC], BF16)
    vcon = nc.dram_tensor("vcon", [8 * TPC, 128], BF16)
    kg = nc.dram_tensor("kg", [8 * 1024, TPC], BF16)
    vg = nc.dram_tensor("vg", [8 * 8 * TPC, 128], BF16)

    plan = {e: [] for e in ENGINES}
    C = {}

    def emit(eng, make, waits=(), incs=()):
        plan[eng].append((make, tuple((s, v) for (s, v) in waits if v > 0), tuple(incs)))
        for s, a in incs:
            C[s] = C.get(s, 0) + a

    def cnt(s):
        return C.get(s, 0)

    def dma(eng, sem, out, in_, waits=()):
        emit(eng, (lambda e, o=out, i=in_: e.dma_start(out=o, in_=i)), waits, ((sem, 16),))
        return (sem, cnt(sem))

    def mm(out, lhsT, rhs, start, stop, waits=(), inc=False):
        emit("tensor",
             (lambda e, o=out, l=lhsT, r=rhs, st=start, sp=stop:
              e.matmul(o, l, r, start=st, stop=sp, skip_group_check=True)),
             waits, ((("pe", 1),) if inc else ()))
        return cnt("pe")

    def act(out, in_, func, bias=0.0, scale=1.0, waits=(), inc=True):
        emit("scalar",
             (lambda e, o=out, i=in_, f=func, b=bias, s=scale:
              e.activation(o, i, f, bias=b, scale=s)),
             waits, ((("act", 1),) if inc else ()))
        return cnt("act")

    def tt(out, a, b, op, waits=(), inc=False):
        emit("vector", (lambda e, o=out, x=a, y=b, p=op: e.tensor_tensor(o, x, y, p)),
             waits, ((("dve", 1),) if inc else ()))
        return cnt("dve")

    def ts(out, in_, s1, s2, op0, op1=None, waits=(), inc=False):
        if s2 is None and op1 is None:
            emit("vector", (lambda e, o=out, i=in_, a=s1, p=op0:
                            e.tensor_single_scalar(o, i, a, p)),
                 waits, ((("dve", 1),) if inc else ()))
        else:
            emit("vector",
                 (lambda e, o=out, i=in_, a=s1, b=s2, p0=op0, p1=op1:
                  e.tensor_scalar(o, i, a, b, p0, p1)),
                 waits, ((("dve", 1),) if inc else ()))
        return cnt("dve")

    def stt(out, in0, scalar, in1, op0, op1, waits=(), inc=False):
        emit("vector",
             (lambda e, o=out, i0=in0, s=scalar, i1=in1, p0=op0, p1=op1:
              e.scalar_tensor_tensor(o, i0, s, i1, p0, p1)),
             waits, ((("dve", 1),) if inc else ()))
        return cnt("dve")

    def cp(out, in_, waits=(), inc=False):
        emit("vector", (lambda e, o=out, i=in_: e.tensor_copy(o, i)),
             waits, ((("dve", 1),) if inc else ()))
        return cnt("dve")

    def ptt(out, a, b, op, waits=(), inc=False):
        emit("gpsimd", (lambda e, o=out, x=a, y=b, p=op: e.tensor_tensor(o, x, y, p)),
             waits, ((("pool", 1),) if inc else ()))
        return cnt("pool")

    def vrecip(out, in_, waits=(), inc=True):
        emit("vector", (lambda e, o=out, i=in_: e.reciprocal(o, i)),
             waits, ((("dve", 1),) if inc else ()))
        return cnt("dve")

    def vred(out, in_, op, waits=(), inc=False):
        emit("vector", (lambda e, o=out, i=in_, p=op: e.tensor_reduce(o, i, AX.X, p)),
             waits, ((("dve", 1),) if inc else ()))
        return cnt("dve")

    def memset(ap_, val):
        emit("vector", (lambda e, a=ap_, v=val: e.memset(a, v)), (), ())

    def iota(out, pattern, base, chan, waits=()):
        emit("gpsimd",
             (lambda e, o=out, p=pattern, b=base, c=chan:
              e.iota(o, p, base=b, channel_multiplier=c,
                     allow_small_or_imprecise_dtypes=True)),
             waits, ((("gi", 1),)))
        return cnt("gi")

    def wait_only(eng, waits):
        emit(eng, None, waits, ())

    with ExitStack() as st:
        E = st.enter_context
        xhat = E(nc.sbuf_tensor("xhat", [128, 32768], BF16))
        xcast = E(nc.sbuf_tensor("xcast", [128, 32768], BF16))
        ar = E(nc.sbuf_tensor("ar", [128, 20480], BF16))
        cosk = E(nc.sbuf_tensor("cosk", [128, TPC], F32))
        posr = E(nc.sbuf_tensor("posr_sb", [1, TPC], F32))
        xc2 = E(nc.sbuf_tensor("xc2", [128, 2048], F32))
        yt2 = E(nc.sbuf_tensor("yt2", [128, 2048], F32))
        sq2 = E(nc.sbuf_tensor("sq2", [128, 2048], BF16))
        qm2 = E(nc.sbuf_tensor("qm2", [1, 32], F32))
        km2 = E(nc.sbuf_tensor("km2", [1, 8], F32))
        tsc = E(nc.sbuf_tensor("tsc", [1, 8], F32))
        negc0 = E(nc.sbuf_tensor("negc0", [128, 8], F32))
        wcol = E(nc.sbuf_tensor("wcol", [128, 2], F32))
        cs16 = E(nc.sbuf_tensor("cs16", [16, 2], F32))
        w16 = E(nc.sbuf_tensor("w16", [16, 1], F32))
        selw = E(nc.sbuf_tensor("selw", [16, 128], F32))
        selw2 = E(nc.sbuf_tensor("selw2", [16, 128], F32))
        iwork = E(nc.sbuf_tensor("iwork", [128, 128], F32))
        iotac = E(nc.sbuf_tensor("iotac", [128, 1], F32))
        iotam = E(nc.sbuf_tensor("iotam", [128, 128], F32))
        idb = E(nc.sbuf_tensor("idb", [128, 128], BF16))
        onesb = E(nc.sbuf_tensor("onesb", [128, 1], BF16))
        onesf = E(nc.sbuf_tensor("onesf", [128, 1], F32))
        ones1f = E(nc.sbuf_tensor("ones1f", [1, 128], F32))
        ones1b = E(nc.sbuf_tensor("ones1b", [1, 512], BF16))
        brow = E(nc.sbuf_tensor("brow", [1, 256], BF16))
        invf = E(nc.sbuf_tensor("invf", [1, 64], F32))
        pacc = E(nc.sbuf_tensor("pacc", [128, 512], F32))
        recipb = E(nc.sbuf_tensor("recipb", [128, 512], F32))

        PSp0 = E(nc.psum_tensor("psp0", [128, 1024], F32))
        PSp1 = E(nc.psum_tensor("psp1", [128, 1024], F32))
        PSp2 = E(nc.psum_tensor("psp2", [128, 1024], F32))
        PSc = E(nc.psum_tensor("psc", [128, 512], F32))
        PSr = E(nc.psum_tensor("psr", [128, 512], F32))
        YP = (PSp0, PSp1)

        sems = {}
        for s in ("gx", "gi", "gw0", "gw1", "gb0", "gb1", "sx0", "sx1", "skv",
                  "sq", "sc", "so0", "so1", "cc", "pe", "act", "dve", "pool"):
            sems[s] = E(nc.semaphore(s))

        M = {}

        def xh(oc, nt=None):
            if nt is None:
                return xhat[:, oc * 1024:(oc + 1) * 1024]
            return xhat[:, oc * 1024 + nt * 512: oc * 1024 + (nt + 1) * 512]

        def xcs(oc, nt=None):
            if nt is None:
                return xcast[:, oc * 1024:(oc + 1) * 1024]
            return xcast[:, oc * 1024 + nt * 512: oc * 1024 + (nt + 1) * 512]

        def arc(c0, c1, r0=0, r1=128):
            return ar[r0:r1, c0:c1]

        def ck(ph):
            if not on(ph):
                raise _StopEmit()
        try:
            dma("sync", "sc", posr[:, :], posD[0:1, :])
            dma("sync", "sc", cs16[:, 0:1], bass.AP(cfD, 0, [[1, 16], [1, 1]]))
            sc_all = dma("sync", "sc", cs16[:, 1:2],
                         bass.AP(scD, 0, [[1, 16], [1, 1]]))

            gi1 = iota(iwork[:, 0:128], [[1, 128]], 0, -1)
            gi2 = iota(iotac[:, 0:1], [[0, 1]], 0, 1)
            gi3 = iota(invf[0:1, 0:64], [[1, 64]], 0, 0)
            gi4 = iota(iotam[:, :], [[0, 128]], 0, 1)

            memset(qm2[0:1, :], 0.0)
            memset(km2[0:1, :], 0.0)
            memset(tsc[0:1, :], 0.0)
            memset(onesb[:, :], 1.0)
            memset(onesf[:, :], 1.0)
            memset(ones1f[:, :], 1.0)
            memset(ones1b[:, :], 1.0)
            m_id = ts(idb[:, :], iwork[:, 0:128], 0.0, None, ALU.is_equal,
                      waits=[("gi", gi1)], inc=True)

            m_invf = act(invf[0:1, :], invf[0:1, :], AF.Exp, bias=0.0,
                         scale=-math.log(10000.0) / 64.0, waits=[("gi", gi3)])
            m_ang = None
            for nt in range(2):
                m_ang = mm(PSp0[0:64, nt * 512:(nt + 1) * 512], invf[0:1, :],
                           posr[0:1, nt * 512:(nt + 1) * 512], True, True,
                           waits=[("act", m_invf), sc_all], inc=True)
            cp(xc2[0:64, 0:1024], PSp0[0:64, :], waits=[("pe", m_ang)])
            ts(yt2[0:64, 0:1024], xc2[0:64, 0:1024], INV_2PI, MAGIC, ALU.mult, ALU.add)
            ts(yt2[0:64, 0:1024], yt2[0:64, 0:1024], -MAGIC, None, ALU.add)
            stt(xc2[64:128, 0:1024], yt2[0:64, 0:1024], -CW_C1, xc2[0:64, 0:1024],
                ALU.mult, ALU.add)
            cp(yt2[64:128, 0:1024], xc2[64:128, 0:1024])
            ts(xc2[64:128, 0:1024], yt2[0:64, 0:1024], CW_C2, None, ALU.mult)
            tt(yt2[64:128, 0:1024], yt2[64:128, 0:1024], xc2[64:128, 0:1024],
               ALU.subtract)
            ts(yt2[64:128, 0:1024], yt2[64:128, 0:1024], math.pi, None, ALU.min)
            m_r = ts(yt2[64:128, 0:1024], yt2[64:128, 0:1024], -math.pi, None,
                     ALU.max, inc=True)
            m_sin = act(cosk[64:128, :], yt2[64:128, 0:1024], AF.Sin,
                        waits=[("dve", m_r)])
            m_abs = act(xc2[64:128, 0:1024], yt2[64:128, 0:1024], AF.Abs,
                        waits=[("dve", m_r)])
            m_t = ts(xc2[64:128, 0:1024], xc2[64:128, 0:1024], -1.0, HALF_PI,
                     ALU.mult, ALU.add, waits=[("act", m_abs)], inc=True)
            m_cos = act(cosk[0:64, :], xc2[64:128, 0:1024], AF.Sin, waits=[("dve", m_t)])

            m_w16 = tt(w16[:, 0:1], cs16[:, 0:1], cs16[:, 1:2], ALU.mult,
                       waits=[sc_all], inc=True)
            m_wcol = None
            for erc in range(2):
                gie = iota(iwork[0:16, 0:128], [[1, 128]], 128 * erc, -16,
                           waits=[("dve", m_id if erc == 0 else cnt("dve"))])
                ts(selw[:, :], iwork[0:16, 0:128], 0.0, None, ALU.is_ge,
                   waits=[("gi", gie)])
                ts(selw2[:, :], iwork[0:16, 0:128], 16.0, None, ALU.is_lt)
                m_sel = tt(selw[:, :], selw[:, :], selw2[:, :], ALU.mult, inc=True)
                m_selmm = mm(PSc[:, 0:1], selw[:, :], w16[:, 0:1], True, True,
                             waits=[("dve", max(m_sel, m_w16))], inc=True)
                m_wcol = cp(wcol[:, erc:erc + 1], PSc[:, 0:1],
                            waits=[("pe", m_selmm)], inc=True)

            def ln_finalize_and_bcast(pe_stats_ms):
                ts(xc2[0:1, 1024:1536], PSp2[0:1, 0:512], 1.0 / LN_N, None, ALU.mult,
                   waits=[("pe", pe_stats_ms)])
                ts(xc2[0:1, 1536:2048], PSp2[0:1, 512:1024], 1.0 / LN_N, None, ALU.mult)
                tt(xc2[0:1, 0:1024], xc2[0:1, 1024:2048], xc2[0:1, 1024:2048], ALU.mult)
                stt(recipb[0:1, :], PSr[0:1, :], 1.0 / LN_N, xc2[0:1, 0:512],
                    ALU.mult, ALU.subtract)
                stt(pacc[0:1, :], PSc[0:1, :], 1.0 / LN_N,
                    xc2[0:1, 512:1024], ALU.mult, ALU.subtract)
                ts(recipb[0:1, :], recipb[0:1, :], EPS, None, ALU.add)
                m_v = ts(pacc[0:1, :], pacc[0:1, :], EPS, None, ALU.add, inc=True)
                act(recipb[0:1, :], recipb[0:1, :], AF.Sqrt, bias=0.0, scale=1.0,
                    waits=[("dve", m_v)])
                m_sq = act(pacc[0:1, :], pacc[0:1, :], AF.Sqrt, bias=0.0, scale=1.0)
                vrecip(xc2[0:1, 0:512], recipb[0:1, :], waits=[("act", m_sq)])
                m_rr = vrecip(xc2[0:1, 512:1024], pacc[0:1, :])
                mm(PSp2[:, 0:512], ones1f[0:1, :], xc2[0:1, 1024:1536], True, True,
                   waits=[("dve", m_rr)])
                mm(PSp2[:, 512:1024], ones1f[0:1, :], xc2[0:1, 1536:2048], True, True)
                mm(PSc[:, :], ones1f[0:1, :], xc2[0:1, 0:512], True, True)
                b4 = mm(PSr[:, :], ones1f[0:1, :], xc2[0:1, 512:1024],
                        True, True, inc=True)
                return b4

            def ln_normalize(src_fn, dst_fn, b4):
                m_norm = None
                for oc in range(NCH):
                    w = [("pe", b4)] if oc == 0 else ()
                    tt(yt2[:, 0:512], src_fn(oc, 0), PSp2[:, 0:512], ALU.subtract, waits=w)
                    tt(yt2[:, 512:1024], src_fn(oc, 1), PSp2[:, 512:1024], ALU.subtract)
                    tt(dst_fn(oc, 0), yt2[:, 0:512], PSc[:, :], ALU.mult)
                    m_norm = tt(dst_fn(oc, 1), yt2[:, 512:1024], PSr[:, :],
                                ALU.mult, inc=True)
                return m_norm

            ck("ln1")
            for oc in range(NCH):
                gx_all = dma("gpsimd", "gx", xcs(oc),
                             xin[oc * 128:(oc + 1) * 128, :])
            m_stats = None
            for oc in range(NCH):
                sl = oc % 2
                w_sq = []
                if oc == 0:
                    w_sq = [gx_all]
                elif oc >= 2:
                    w_sq = [("pe", M[("ln1s", oc - 2)])]
                m_sqr = act(sq2[:, sl * 1024:(sl + 1) * 1024], xcs(oc), AF.Square,
                            waits=w_sq)
                first = (oc == 0)
                last = (oc == NCH - 1)
                mm(PSp2[0:1, 0:512], onesb[:, :], xcs(oc, 0), first, last,
                   waits=[gx_all] if first else ())
                mm(PSp2[0:1, 512:1024], onesb[:, :], xcs(oc, 1), first, last)
                mm(PSr[0:1, :], onesb[:, :], sq2[:, sl * 1024: sl * 1024 + 512],
                   first, last, waits=[("act", m_sqr)])
                m_stats = mm(PSc[0:1, :], onesb[:, :],
                             sq2[:, sl * 1024 + 512:(sl + 1) * 1024], first, last,
                             inc=True)
                M[("ln1s", oc)] = m_stats
            m_bc1 = ln_finalize_and_bcast(m_stats)
            m_ln1 = ln_normalize(xcs, xh, m_bc1)

            def emit_prep_loads(first_wait_pe):
                for kc in range(NCH):
                    dma("gpsimd", "gw0", arc(AT0 + kc * 256, AT0 + (kc + 1) * 256),
                        atm[kc * 128:(kc + 1) * 128, :],
                        waits=[("pe", first_wait_pe)] if kc == 0 and first_wait_pe else ())
                gw = None
                for erc in range(2):
                    gw = dma("gpsimd", "gw0",
                             arc(BST0 + erc * 4096, BST0 + (erc + 1) * 4096),
                             bstm[erc * 128:(erc + 1) * 128, :])
                return gw

            def emit_xa(src_fn, gw_prep, dep_dve):
                m_xa = None
                for erc in range(2):
                    mms = [None, None]
                    for nt in range(2):
                        ps = PSc if nt == 0 else PSr
                        for kc in range(NCH):
                            w = []
                            if erc == 0 and nt == 0 and kc == 0:
                                w = [gw_prep, ("dve", dep_dve)]
                            if erc == 1 and kc == 0:
                                w = [("dve", m_xa)]
                            mms[nt] = mm(ps[:, :],
                                         arc(AT0 + kc * 256 + erc * 128,
                                             AT0 + kc * 256 + (erc + 1) * 128),
                                         src_fn(kc, nt), kc == 0, kc == NCH - 1,
                                         waits=w, inc=(kc == NCH - 1))
                    for nt in range(2):
                        ps = PSc if nt == 0 else PSr
                        m_xa = ts(arc(XA0 + erc * 1024 + nt * 512,
                                      XA0 + erc * 1024 + (nt + 1) * 512),
                                  ps[:, :], wcol[:, erc:erc + 1], None, ALU.mult,
                                  waits=[("pe", mms[nt]), ("dve", m_wcol)], inc=True)
                return m_xa

            def proj_loop(tag, n_oc, wdram, wcols, bdram, rhs_fn, lowrank, scale_half,
                          res_in, res_out, res_wait, cast_fn, fold_stats, final_out,
                          rope, to_stage, first_waits):
                M[(tag, "bankfree", 0)] = [("dve", cnt("dve"))]
                M[(tag, "bankfree", 1)] = [("dve", cnt("dve"))]
                pe_now = cnt("pe")

                def wslab_ap(p):
                    return wdram[p * 128:(p + 1) * 128, :]

                for p in range(min(2, n_oc)):
                    M[(tag, "wl", p)] = dma(
                        "gpsimd", "gw%d" % (p % 2), arc(WS0 if p % 2 == 0 else WS1,
                                            (WS0 if p % 2 == 0 else WS1) + 4096),
                        wslab_ap(p),
                        waits=(list(first_waits) + [("pe", pe_now)]) if p == 0 else ())
                    M[(tag, "bl", p)] = dma(
                        "gpsimd", "gb%d" % (p % 2),
                        brow[0:1, (p % 2) * 128:(p % 2 + 1) * 128],
                        bass.AP(bdram, p * 128, [[1, 1], [1, 128]]),
                        waits=[("pe", pe_now)] if p == 0 else ())
                    if res_in is not None:
                        M[(tag, "xl", p)] = dma(
                            "sync", "sx%d" % (p % 2),
                            xc2[:, (p % 2) * 1024:(p % 2 + 1) * 1024],
                            res_in[p * 128:(p + 1) * 128, :],
                            waits=list(res_wait) + [("pe", pe_now)])
                m_last = None
                for oc in range(n_oc):
                    par = oc % 2
                    ps = YP[par]
                    slab0 = WS0 if par == 0 else WS1
                    ymm = [None, None]
                    for nt in range(2):
                        for kc in range(NCH):
                            w = []
                            if kc == 0 and nt == 0:
                                w = [M[(tag, "wl", oc)], M[(tag, "bl", oc)]]
                                w += M[(tag, "bankfree", par)]
                            mm(ps[:, nt * 512:(nt + 1) * 512],
                               arc(slab0 + kc * 128, slab0 + (kc + 1) * 128),
                               rhs_fn(kc, nt), kc == 0, False, waits=w)
                        if lowrank:
                            for erc in range(2):
                                mm(ps[:, nt * 512:(nt + 1) * 512],
                                   arc(BST0 + erc * 4096 + oc * 128,
                                       BST0 + erc * 4096 + (oc + 1) * 128),
                                   arc(XA0 + erc * 1024 + nt * 512,
                                       XA0 + erc * 1024 + (nt + 1) * 512),
                                   False, False)
                        ymm[nt] = mm(ps[:, nt * 512:(nt + 1) * 512],
                                     brow[0:1, par * 128:(par + 1) * 128],
                                     ones1b[0:1, 0:512], False, True, inc=True)
                    M[(tag, "ymm", oc)] = ymm[1]

                    if rope is not None:
                        scale = INVSQD if rope == "q" else 1.0
                        stg = arc(QS0 + par * 1024, QS0 + (par + 1) * 1024)
                        w0 = [("pe", ymm[1])]
                        if oc >= 2:
                            w0.append(M[(tag, "st", oc - 2)])
                        stt(xc2[0:64, 0:1024], ps[0:64, :], scale, cosk[0:64, :],
                            ALU.mult, ALU.mult, waits=w0)
                        stt(yt2[0:64, 0:1024], ps[64:128, :], scale, cosk[64:128, :],
                            ALU.mult, ALU.mult)
                        tt(stg[0:64, :], xc2[0:64, 0:1024], yt2[0:64, 0:1024],
                           ALU.subtract)
                        stt(xc2[0:64, 0:1024], ps[0:64, :], scale, cosk[64:128, :],
                            ALU.mult, ALU.mult)
                        stt(yt2[0:64, 0:1024], ps[64:128, :], scale, cosk[0:64, :],
                            ALU.mult, ALU.mult)
                        m_rope = tt(stg[64:128, :], xc2[0:64, 0:1024],
                                    yt2[0:64, 0:1024], ALU.add, inc=True)
                        M[(tag, "bankfree", par)] = [("dve", m_rope)]
                        pend0 = M.setdefault((tag, "qnpend"), [])
                        if len(pend0) >= 2:
                            emit_qn(*pend0.pop(0))
                        w_sq = [("dve", m_rope)]
                        if oc >= 2:
                            w_sq.append(("pe", M[(tag, "qn", oc - 2)]))
                        m_sqr = act(sq2[:, par * 1024:(par + 1) * 1024], stg[:, :],
                                    AF.Square, waits=w_sq)

                        def emit_qn(oc2, m_sqr2, par2):
                            w0 = [("act", m_sqr2)]
                            if M.get((tag, "nrmlast")):
                                w0.append(("dve", M[(tag, "nrmlast")]))
                            mm(PSp2[0:1, 0:512], onesb[:, :],
                               sq2[:, par2 * 1024:par2 * 1024 + 512], True, True,
                               waits=w0)
                            n2 = mm(PSp2[0:1, 512:1024], onesb[:, :],
                                    sq2[:, par2 * 1024 + 512:(par2 + 1) * 1024],
                                    True, True, inc=True)
                            M[(tag, "qn", oc2)] = n2
                            dstq = qm2 if rope == "q" else km2
                            vred(dstq[0:1, oc2:oc2 + 1], PSp2[0:1, 0:512], ALU.max,
                                 waits=[("pe", n2)])
                            memset(tsc[0:1, 0:1], 0.0)
                            vred(tsc[0:1, 0:1], PSp2[0:1, 512:1024], ALU.max)
                            m_nrm = tt(dstq[0:1, oc2:oc2 + 1],
                                       dstq[0:1, oc2:oc2 + 1],
                                       tsc[0:1, 0:1], ALU.max, inc=True)
                            M[(tag, "nrm", oc2)] = m_nrm
                            M[(tag, "nrmlast")] = m_nrm
                            return m_nrm

                        pend = M[(tag, "qnpend")]
                        pend.append((oc, m_sqr, par))
                        if oc == n_oc - 1:
                            while pend:
                                m_last = emit_qn(*pend.pop(0))
                        M[(tag, "st", oc)] = to_stage(oc, stg, m_rope)
                    elif to_stage is not None:
                        stg = arc(QS0 + par * 1024, QS0 + (par + 1) * 1024)
                        w0 = [("pe", ymm[1])]
                        if oc >= 2:
                            w0.append(("pe", M[(tag, "tms", oc - 2)]))
                        cp(stg[:, 0:512], ps[:, 0:512], waits=w0)
                        m_ev = cp(stg[:, 512:1024], ps[:, 512:1024], inc=True)
                        M[(tag, "bankfree", par)] = [("dve", m_ev)]
                        tms = None
                        for tti in range(8):
                            psd = PSc if tti < 4 else PSr
                            w = []
                            if tti == 0:
                                w = [("dve", m_ev)]
                                if oc >= 1:
                                    w.append(("dve", M[(tag, "vtm", oc - 1)]))
                            tms = mm(psd[:, (tti % 4) * 128:(tti % 4 + 1) * 128],
                                     stg[:, tti * 128:(tti + 1) * 128], idb[:, :],
                                     True, True, waits=w, inc=(tti == 7))
                        M[(tag, "tms", oc)] = tms
                        vst = arc(VT0 + par * 1024, VT0 + (par + 1) * 1024)
                        w = [("pe", tms)]
                        if oc >= 2:
                            w.append(M[(tag, "st", oc - 2)])
                        cp(vst[:, 0:512], PSc[:, :], waits=w)
                        m_vtm = cp(vst[:, 512:1024], PSr[:, :], inc=True)
                        M[(tag, "vtm", oc)] = m_vtm
                        M[(tag, "st", oc)] = to_stage(oc, vst, m_vtm)
                        m_last = m_vtm
                    else:
                        for nt in range(2):
                            w = [("pe", ymm[nt]), M[(tag, "xl", oc)]]
                            if oc >= 2:
                                w.append(M[(tag, "st", oc - 2)])
                            M[(tag, "stt", oc, nt)] = stt(
                                yt2[:, par * 1024 + nt * 512: par * 1024 + (nt + 1) * 512],
                                ps[:, nt * 512:(nt + 1) * 512], scale_half,
                                xc2[:, par * 1024 + nt * 512: par * 1024 + (nt + 1) * 512],
                                ALU.mult, ALU.add, waits=w, inc=True)
                        M[(tag, "bankfree", par)] = [("dve", M[(tag, "stt", oc, 1)])]
                        dst = outD if final_out else res_out
                        M[(tag, "st", oc)] = dma(
                            "sync", "so%d" % par, dst[oc * 128:(oc + 1) * 128, :],
                            yt2[:, par * 1024:(par + 1) * 1024],
                            waits=[("dve", M[(tag, "stt", oc, 1)])])
                        if fold_stats:
                            m_cast = cp(cast_fn(oc), yt2[:, par * 1024:(par + 1) * 1024],
                                        inc=True)
                            w_sq = [("dve", m_cast)]
                            if oc >= 2:
                                w_sq.append(("pe", M[(tag, "lns", oc - 2)]))
                            m_sqr = act(sq2[:, par * 1024:(par + 1) * 1024],
                                        cast_fn(oc), AF.Square, waits=w_sq)
                            first = (oc == 0)
                            last = (oc == n_oc - 1)
                            mm(PSp2[0:1, 0:512], onesb[:, :], cast_fn(oc, 0), first, last,
                               waits=[("dve", m_cast)])
                            mm(PSp2[0:1, 512:1024], onesb[:, :], cast_fn(oc, 1), first, last)
                            mm(PSr[0:1, :], onesb[:, :],
                               sq2[:, par * 1024: par * 1024 + 512], first, last,
                               waits=[("act", m_sqr)])
                            M[(tag, "lns", oc)] = mm(
                                PSc[0:1, :], onesb[:, :],
                                sq2[:, par * 1024 + 512:(par + 1) * 1024], first, last,
                                inc=True)
                            m_last = M[(tag, "lns", oc)]
                        else:
                            m_last = M[(tag, "stt", oc, 1)]
                    if oc + 2 < n_oc:
                        p = oc + 2
                        M[(tag, "wl", p)] = dma(
                            "gpsimd", "gw%d" % (p % 2), arc(WS0 if p % 2 == 0 else WS1,
                                                (WS0 if p % 2 == 0 else WS1) + 4096),
                            wslab_ap(p), waits=[("pe", ymm[1])])
                        M[(tag, "bl", p)] = dma(
                            "gpsimd", "gb%d" % (p % 2),
                            brow[0:1, (p % 2) * 128:(p % 2 + 1) * 128],
                            bass.AP(bdram, p * 128, [[1, 1], [1, 128]]),
                            waits=[("pe", ymm[1])])
                        if res_in is not None:
                            M[(tag, "xl", p)] = dma(
                                "sync", "sx%d" % (p % 2),
                                xc2[:, (p % 2) * 1024:(p % 2 + 1) * 1024],
                                res_in[p * 128:(p + 1) * 128, :],
                                waits=list(res_wait) + [("dve", M[(tag, "stt", oc, 1)])])
                return m_last

            ck("xa")
            gw_prep = emit_prep_loads(0)
            m_xa1 = emit_xa(lambda kc, nt: xh(kc, nt), gw_prep, m_ln1)
            ck("f1")
            m_f1 = proj_loop("f1", f1_noc, wft, DM, bf_, lambda kc, nt: xh(kc, nt),
                             True, 0.5, xin, xr1, (), xcs, True, False, None, None,
                             first_waits=[("dve", m_xa1)])
            ck("ln2")
            m_bc2 = ln_finalize_and_bcast(m_f1)
            m_ln2 = ln_normalize(xcs, xh, m_bc2)

            ck("k")
            def k_store(oc, stg, m_rope):
                return dma("sync", "so%d" % (oc % 2),
                           kcon[oc * 128:(oc + 1) * 128, :], stg[:, :],
                           waits=[("dve", m_rope)])

            proj_loop("k", 8, wk, 1024, bk, lambda kc, nt: xh(kc, nt),
                      False, 1.0, None, None, (), None, False, False, "k", k_store,
                      first_waits=[("dve", m_ln2)])

            so_k = [("so0", cnt("so0")), ("so1", cnt("so1"))]
            emit("gpsimd",
                 (lambda e: e.collective_compute(
                     "AllGather", ALU.bypass, replica_groups=[list(range(8))],
                     ins=[kcon.ap().opt()], outs=[kg.ap().opt()])),
                 so_k, ((("cc", 1),)))
            ck("v")
            def v_store(oc, vst, m_vtm):
                last = None
                for tti in range(8):
                    last = dma("sync", "so%d" % (oc % 2),
                               vcon[(oc * 8 + tti) * 128:(oc * 8 + tti + 1) * 128, :],
                               vst[:, tti * 128:(tti + 1) * 128],
                               waits=[("dve", m_vtm)] if tti == 0 else ())
                return last

            proj_loop("v", 8, wv, 1024, bv, lambda kc, nt: xh(kc, nt),
                      False, 1.0, None, None, (), None, False, False, None, v_store,
                      first_waits=())

            ck("cc")
            so_v = [("so0", cnt("so0")), ("so1", cnt("so1"))]
            emit("gpsimd",
                 (lambda e: e.collective_compute(
                     "AllGather", ALU.bypass, replica_groups=[list(range(8))],
                     ins=[vcon.ap().opt()], outs=[vg.ap().opt()])),
                 so_v, ((("cc", 1),)))
            cc_all = cnt("cc")

            ck("q")
            def q_store(oc, stg, m_rope):
                return dma("sync", "so%d" % (oc % 2),
                           qd[:, oc * 1024:(oc + 1) * 1024], stg[:, :],
                           waits=[("dve", m_rope)])

            proj_loop("q", 32, wq, DM, bq, lambda kc, nt: xh(kc, nt),
                      False, 1.0, None, None, (), None, False, False, "q", q_store,
                      first_waits=())

            so_all = [("so0", cnt("so0")), ("so1", cnt("so1"))]
            m_nrm_all = max(max(M[("q", "nrm", o)] for o in range(32)),
                            max(M[("k", "nrm", o)] for o in range(8)))
            m_negc = None
            for kvh in range(8):
                memset(tsc[0:1, 0:1], 0.0)
                vred(tsc[0:1, 0:1], qm2[0:1, 4 * kvh:4 * kvh + 4], ALU.max,
                     waits=[("dve", m_nrm_all)] if kvh == 0 else ())
                m_t2 = tt(tsc[0:1, 1:2], tsc[0:1, 0:1], km2[0:1, kvh:kvh + 1],
                          ALU.mult, inc=True)
                m_sq3 = act(tsc[0:1, 2:3], tsc[0:1, 1:2], AF.Sqrt,
                            waits=[("dve", m_t2)])
                m_t4 = ts(tsc[0:1, 3:4], tsc[0:1, 2:3], 1.0, -1.0, ALU.add, ALU.mult,
                          waits=[("act", m_sq3)], inc=True)
                m_bcn = mm(PSc[:, 0:1], ones1f[0:1, :], tsc[0:1, 3:4], True, True,
                           waits=[("dve", m_t4)], inc=True)
                m_negc = cp(negc0[:, kvh:kvh + 1], PSc[:, 0:1],
                            waits=[("pe", m_bcn)], inc=True)

            ck("p3")
            m_mask = None
            for g in range(NG):
                mb_ = mm(PSc[:, 0:128], ones1f[0:1, :],
                         posr[0:1, g * 128:(g + 1) * 128], True, True,
                         waits=[("dve", m_negc if g == 0 else cnt("dve"))], inc=True)
                for mp in range(8):
                    w = [("pe", mb_)] if mp == 0 else ()
                    if mp == 0:
                        w = list(w) + [("gi", gi4)]
                    ts(iwork[:, 0:128], PSc[:, 0:128],
                       -float(1024 * g + 128 * mp), None, ALU.add, waits=w)
                    tt(iwork[:, 0:128], iwork[:, 0:128], iotam[:, :], ALU.subtract)
                    m_mask = ts(arc(MK0 + (g * 8 + mp) * 128,
                                    MK0 + (g * 8 + mp + 1) * 128),
                                iwork[:, 0:128], 0.0, None, ALU.is_ge, inc=True)

            PSPS = (PSp0, PSp1, PSp2)
            NRING = 6
            pair_ctr = 0
            exp_bank = [0, 0, 0]
            ring_ctx = [0] * NRING
            ring_sum = [0] * NRING
            den_pend = []
            m_ev = 0
            ps_merge = [0, 0]
            m_ctx_evac = 0
            m_acc = 0
            def kv_loads(kvh2, waits0):
                base = (kvh2 % 2) * 16384
                dma("sync", "skv",
                    bass.AP(xhat, base, [[32768, 128], [1024, 8], [1, 1024]]),
                    bass.AP(kg, kvh2 * 128 * TPC,
                            [[TPC, 128], [1024 * TPC, 8], [1, 1024]]),
                    waits=waits0)
                r = None
                for c in range(8):
                    r = dma("sync", "skv",
                            bass.AP(xhat, base + 8192 + c * 1024,
                                    [[32768, 128], [128, 8], [1, 128]]),
                            bass.AP(vg, (c * 8 * TPC + kvh2 * TPC) * 128,
                                    [[128, 128], [128 * 128, 8], [1, 128]]))
                return r

            kv_pend = {0: kv_loads(0, [("cc", cc_all)]),
                       1: kv_loads(1, ())}
            prev_pe_done = {}
            for kvh in range(8):
                Kbase = (kvh % 2) * 16384
                Vbase = Kbase + 8192
                kv_ms = kv_pend[kvh % 2]
                q_ms = dma("sync", "sq", arc(QB0, QB0 + 4096),
                           qd[:, kvh * 4096:(kvh + 1) * 4096],
                           waits=[("pe", prev_pe_done.get(kvh - 1, 0))]
                           if prev_pe_done.get(kvh - 1) else ())
                for g in range(NG):
                    q4 = bass.AP(ar, QB0 + g * 128, [[20480, 128], [1024, 4], [1, 128]])
                    tiles = [(c, kb) for kb in range(g + 1) for c in range(8)]
                    npair = len(tiles) // 2
                    pair_meta = []
                    ctx_ms = 0
                    ctx_state = [False]

                    def emit_ctx_pair(pj2):
                        nonlocal ctx_ms
                        slot2, m_exp2, m_msk2 = pair_meta[pj2]
                        prt = tiles[2 * pj2:2 * pj2 + 2]
                        for j, (c, kb) in enumerate(prt):
                            w = []
                            if j == 0:
                                w = [("act", m_exp2)]
                                if m_msk2:
                                    w.append(("dve", m_msk2))
                                if not ctx_state[0] and m_ctx_evac:
                                    w.append(("dve", m_ctx_evac))
                            ctx_ms = mm(PSc[:, :],
                                        bass.AP(xhat, Vbase + (c * 8 + kb) * 128,
                                                [[32768, 128], [1, 128]]),
                                        arc(PR0 + slot2 * 1024 + j * 512,
                                            PR0 + slot2 * 1024 + (j + 1) * 512),
                                        not ctx_state[0],
                                        pj2 == npair - 1 and j == 1,
                                        waits=w, inc=(j == 1))
                            ctx_state[0] = True
                        ring_ctx[slot2] = ctx_ms

                    for pj in range(npair):
                        prt = tiles[2 * pj:2 * pj + 2]
                        bank = pair_ctr % 3
                        psp = PSPS[bank]
                        slot = pair_ctr % NRING
                        sc_ms = None
                        for j, (c, kb) in enumerate(prt):
                            w = []
                            if pj == 0 and g == 0 and j == 0:
                                w = [kv_ms, q_ms]
                                if kvh == 0:
                                    w.append(("dve", m_mask))
                            if j == 0 and exp_bank[bank]:
                                w.append(("act", exp_bank[bank]))
                            sc_ms = mm(psp[:, j * 512:(j + 1) * 512],
                                       bass.AP(xhat, Kbase + (c * 8 + kb) * 128,
                                               [[32768, 128], [1, 128]]),
                                       q4, True, True, waits=w, inc=(j == 1))
                        w = [("pe", sc_ms)]
                        if ring_ctx[slot]:
                            w.append(("pe", ring_ctx[slot]))
                        if ring_sum[slot]:
                            w.append(("dve", ring_sum[slot]))
                        m_exp = act(arc(PR0 + slot * 1024, PR0 + (slot + 1) * 1024),
                                    psp[:, :], AF.Exp, bias=negc0[:, kvh:kvh + 1],
                                    scale=1.0, waits=w)
                        exp_bank[bank] = m_exp
                        m_msk = 0
                        for j, (c, kb) in enumerate(prt):
                            if kb == g:
                                mp = (c + g) % 8
                                sbase = PR0 + slot * 1024 + j * 512
                                m_msk = tt(arc(sbase, sbase + 512),
                                           arc(sbase, sbase + 512),
                                           bass.AP(ar, MK0 + (g * 8 + mp) * 128,
                                                   [[20480, 128], [0, 4], [1, 128]]),
                                           ALU.mult,
                                           waits=[("act", m_exp)] if m_msk == 0 else (),
                                           inc=True)
                        pair_meta.append((slot, m_exp, m_msk))
                        w = [("act", m_exp)]
                        m_psum = tt(arc(PSM0 + (pj % 2) * 512,
                                        PSM0 + (pj % 2 + 1) * 512),
                                    arc(PR0 + slot * 1024, PR0 + slot * 1024 + 512),
                                    arc(PR0 + slot * 1024 + 512,
                                        PR0 + (slot + 1) * 1024),
                                    ALU.add, waits=w, inc=True)
                        ring_sum[slot] = m_psum
                        if pj >= 4:
                            emit_ctx_pair(pj - 4)
                        if pj == 2 and den_pend:
                            m_ev = den_tail(*den_pend.pop(0))
                        if pj % 2 == 1:
                            tt(arc(QDB0, QDB0 + 512), arc(PSM0, PSM0 + 512),
                               arc(PSM0 + 512, PSM0 + 1024), ALU.add)
                            if pj == 1:
                                m_acc = cp(pacc[:, :], arc(QDB0, QDB0 + 512), inc=True)
                            else:
                                m_acc = tt(pacc[:, :], pacc[:, :],
                                           arc(QDB0, QDB0 + 512), ALU.add, inc=True)
                            ps_merge[0] = m_acc
                            ps_merge[1] = m_acc
                        pair_ctr += 1
                    for pj2 in range(max(0, npair - 4), npair):
                        emit_ctx_pair(pj2)
                    m_den = mm(PSr[0:1, :], onesf[:, :], pacc[:, :], True, True,
                               waits=[("dve", m_acc)], inc=True)
                    m_ctx_evac = cp(arc(19968, 20480), PSc[:, :],
                                    waits=[("pe", ctx_ms)], inc=True)
                    ts(xc2[0:1, 0:512], PSr[0:1, :], 1e-38, None, ALU.add,
                       waits=[("pe", m_den)])
                    m_rec = vrecip(xc2[0:1, 512:1024], xc2[0:1, 0:512])

                    def den_tail(kvh2, g2, m_rec2):
                        m_rbc = mm(PSr[:, :], ones1f[0:1, :], xc2[0:1, 512:1024],
                                   True, True, waits=[("dve", m_rec2)], inc=True)
                        m_rcp = cp(recipb[:, :], PSr[:, :], waits=[("pe", m_rbc)],
                                   inc=True)
                        m_ev2 = None
                        for h in range(4):
                            col = (4 * kvh2 + h) * 1024 + g2 * 128
                            m_ev2 = tt(xcast[:, col:col + 128],
                                       arc(19968 + h * 128, 19968 + (h + 1) * 128),
                                       recipb[:, h * 128:(h + 1) * 128], ALU.mult,
                                       inc=(h == 3))
                        return m_ev2
                    den_pend.append((kvh, g, m_rec))
                prev_pe_done[kvh] = cnt("pe")
                if kvh + 2 < 8:
                    kv_pend[kvh % 2] = kv_loads(kvh + 2,
                                                [("pe", prev_pe_done[kvh])])
            while den_pend:
                m_ev = den_tail(*den_pend.pop(0))
            m_attn = m_ev

            ck("p4")
            m_p4 = proj_loop("o", NCH, wo, DM, bo_, lambda kc, nt: xcs(kc, nt),
                             False, 1.0, xr1, xr2, so_all, xh, True, False,
                             None, None, first_waits=[("dve", m_attn)])
            ck("ln3")
            m_bc3 = ln_finalize_and_bcast(m_p4)
            m_ln3 = ln_normalize(xh, xcs, m_bc3)
            so_p4 = [("so0", cnt("so0")), ("so1", cnt("so1"))]

            ck("f2")
            gw_prep2 = emit_prep_loads(cnt("pe"))
            m_xa2 = emit_xa(lambda kc, nt: xcs(kc, nt), gw_prep2, m_ln3)
            proj_loop("f2", NCH, wft, DM, bf_, lambda kc, nt: xcs(kc, nt),
                      True, 0.5, xr2, None, so_p4, None, False, True,
                      None, None, first_waits=[("dve", m_xa2)])

            wait_only("sync", [("so0", cnt("so0")), ("so1", cnt("so1"))])
        except _StopEmit:
            allsems = [(sname, cnt(sname)) for sname in
                       ("pe", "act", "dve", "cc", "gx", "gi", "gw0", "gw1",
                        "gb0", "gb1", "sx0", "sx1", "skv", "sq", "sc",
                        "so0", "so1") if cnt(sname) > 0]
            wait_only("sync", allsems)
            wait_only("gpsimd", allsems)
            if stop_at >= _PHASES.index("p4"):
                dma("sync", "so0", outD[:, :], xr2[:, :])
            elif stop_at >= _PHASES.index("p3"):
                cp(xcast[:, 8:16], negc0[:, :])
                cp(xcast[0:1, 16:48], qm2[0:1, :])
                cp(xcast[0:1, 48:56], km2[0:1, :])
                cp(xcast[0:1, 56:64], tsc[0:1, :], inc=True)
                wait_only("gpsimd", [("dve", cnt("dve"))])
                dma("gpsimd", "so0",
                    bass.AP(outD, 0, [[32768, 128], [1, 32768]]),
                    xcast[:, :])

            elif stop_at >= _PHASES.index("cc"):
                dma("gpsimd", "so0",
                    bass.AP(outD, 0, [[1024, 4096], [1, 1024]]),
                    kg[0:4096, :])
            elif stop_at >= _PHASES.index("q"):
                dma("gpsimd", "so0",
                    bass.AP(outD, 0, [[32768, 128], [1, 32768]]),
                    bass.AP(qd, 0, [[32768, 128], [1, 32768]]))
            elif stop_at >= _PHASES.index("f1"):
                dma("sync", "so0", outD[:, :], xr1[:, :])
            else:
                for oc in range(NCH):
                    dma("sync", "so0", outD[oc * 128:(oc + 1) * 128, :],
                        yt2[:, 0:1024])
            m_dbg = mm(PSc[:, :], ones1f[0:1, :], posr[0:1, 0:512], True, True,
                       inc=True)
            cp(yt2[:, 0:512], PSc[:, :], waits=[("pe", m_dbg)])
            cp(yt2[:, 512:520], negc0[:, :], inc=True)
            wait_only("sync", [("dve", cnt("dve"))])
            dma("sync", "so0", outD[3968:4096, :], yt2[:, 0:1024])
            wait_only("sync", [("so0", cnt("so0")), ("so1", cnt("so1"))])


        block = E(nc.Block())

        def replay(name):
            def f(e):
                for make, waits, incs in plan[name]:
                    for (s, v) in waits:
                        e.wait_ge(sems[s], v)
                    if make is None:
                        continue
                    inst = make(e)
                    for (s, a) in incs:
                        inst = inst.then_inc(sems[s], a)
            return f

        _CACHE["C"] = dict(C)
        block.sync(replay("sync"))
        block.scalar(replay("scalar"))
        block.vector(replay("vector"))
        block.tensor(replay("tensor"))
        block.gpsimd(replay("gpsimd"))

    return nc


def kernel(**inputs):
    if "nc" not in _CACHE:
        _CACHE["nc"] = build()
    nc = _CACHE["nc"]

    f32 = lambda k: np.ascontiguousarray(np.asarray(inputs[k], np.float32))
    x = f32("L0_attn_hidden_states")

    def blockw(w):
        dmo = w.shape[1]
        t = w.reshape(32, 128, dmo // 128, 128)
        return np.ascontiguousarray(t.transpose(2, 1, 0, 3).reshape(dmo, 4096))

    wft = blockw(np.ascontiguousarray(f32("L0_ffn_W").T))
    atm = np.ascontiguousarray(f32("L0_ffn_A").reshape(256, 4096).T)
    bstm = np.ascontiguousarray(f32("L0_ffn_Bf").transpose(0, 2, 1).reshape(256, 4096))
    shared = {
        "wq": blockw(f32("L0_attn_Wq")), "wk": blockw(f32("L0_attn_Wk")),
        "wv": blockw(f32("L0_attn_Wv")),
        "wo": blockw(f32("L0_attn_Wo")), "wft": wft,
        "bq": f32("L0_attn_bq"), "bk": f32("L0_attn_bk"), "bv": f32("L0_attn_bv"),
        "bo": f32("L0_attn_bo"), "bf": f32("L0_ffn_bias"),
        "atm": atm, "bstm": bstm,
        "cf": f32("L0_ffn_coeffs"), "sc": f32("L0_ffn_scales"),
    }
    in_maps = []
    rows_per_core = []
    for i in range(8):
        tiles = [8 * g + (g + i) % 8 for g in range(8)]
        rows = np.concatenate([np.arange(t * 128, (t + 1) * 128) for t in tiles])
        rows_per_core.append(rows)
        in_maps.append(dict(shared,
                            xin=np.ascontiguousarray(x[0, rows, :].T),
                            posr=np.ascontiguousarray(
                                rows.astype(np.float32).reshape(1, 1024))))

    res = run_bass_kernel_spmd(nc, in_maps, core_ids=list(range(8)))

    out = np.empty((1, 8192, 4096), np.float32)
    for i in range(8):
        out[0, rows_per_core[i], :] = res.results[i]["out"].T
    return out


# version 17
# speedup vs baseline: 1.1107x; 1.1107x over previous
import math
from contextlib import ExitStack

import numpy as np

import concourse.bass as bass
import concourse.mybir as mybir
from concourse.bass_utils import run_bass_kernel_spmd

F32 = mybir.dt.float32
BF16 = mybir.dt.bfloat16
AF = mybir.ActivationFunctionType
ALU = mybir.AluOpType
AX = mybir.AxisListType

DM = 4096
TPC = 1024
NCH = 32
NG = 8
INVSQD = 1.0 / math.sqrt(128.0)
EPS = 1e-5
LN_N = float(DM)
TWO_PI = 6.283185307179586
INV_2PI = 1.0 / TWO_PI
CW_C1 = 6.28125
CW_C2 = TWO_PI - CW_C1
MAGIC = 12582912.0
HALF_PI = math.pi / 2.0

ENGINES = ("sync", "scalar", "vector", "tensor", "gpsimd")

XA0 = 0
BST0 = 2048
AT0 = 10240
WS0 = 10240
WS1 = 14336
QS0 = 8192
VT0 = 18432
MK0 = 0
QB0 = 8192
PR0 = 12288
PSM0 = 18432
QDB0 = 19456

_CACHE = {}
_FLAGS = set()


class _StopEmit(Exception):
    pass


_PHASES = ("p0", "ln1", "xa", "f1", "ln2", "k", "v", "cc", "q", "p3", "p4",
           "ln3", "f2")


def build(upto="f2"):
    f1_noc = NCH
    if upto.startswith("f1."):
        f1_noc = int(upto.split(".")[1])
        upto = "f1"
    stop_at = _PHASES.index(upto)

    def on(ph):
        return _PHASES.index(ph) <= stop_at

    nc = bass.Bass("TRN2", target_bir_lowering=False, debug=False)

    xin = nc.dram_tensor("xin", [DM, TPC], F32, kind="ExternalInput")
    wq = nc.dram_tensor("wq", [DM, DM], F32, kind="ExternalInput")
    wk = nc.dram_tensor("wk", [1024, DM], F32, kind="ExternalInput")
    wv = nc.dram_tensor("wv", [1024, DM], F32, kind="ExternalInput")
    wo = nc.dram_tensor("wo", [DM, DM], F32, kind="ExternalInput")
    wft = nc.dram_tensor("wft", [DM, DM], F32, kind="ExternalInput")
    bq = nc.dram_tensor("bq", [DM], F32, kind="ExternalInput")
    bk = nc.dram_tensor("bk", [1024], F32, kind="ExternalInput")
    bv = nc.dram_tensor("bv", [1024], F32, kind="ExternalInput")
    bo_ = nc.dram_tensor("bo", [DM], F32, kind="ExternalInput")
    bf_ = nc.dram_tensor("bf", [DM], F32, kind="ExternalInput")
    atm = nc.dram_tensor("atm", [DM, 256], F32, kind="ExternalInput")
    bstm = nc.dram_tensor("bstm", [256, DM], F32, kind="ExternalInput")
    cfD = nc.dram_tensor("cf", [16], F32, kind="ExternalInput")
    scD = nc.dram_tensor("sc", [16], F32, kind="ExternalInput")
    posD = nc.dram_tensor("posr", [1, TPC], F32, kind="ExternalInput")
    outD = nc.dram_tensor("out", [DM, TPC], F32, kind="ExternalOutput")

    xr1 = nc.dram_tensor("xr1", [DM, TPC], F32)
    xr2 = nc.dram_tensor("xr2", [DM, TPC], F32)
    qd = nc.dram_tensor("qdr", [128, 32 * TPC], BF16)
    kcon = nc.dram_tensor("kcon", [1024, TPC], BF16)
    vcon = nc.dram_tensor("vcon", [8 * TPC, 128], BF16)
    kg = nc.dram_tensor("kg", [8 * 1024, TPC], BF16)
    vg = nc.dram_tensor("vg", [8 * 8 * TPC, 128], BF16)

    plan = {e: [] for e in ENGINES}
    C = {}

    def emit(eng, make, waits=(), incs=()):
        plan[eng].append((make, tuple((s, v) for (s, v) in waits if v > 0), tuple(incs)))
        for s, a in incs:
            C[s] = C.get(s, 0) + a

    def cnt(s):
        return C.get(s, 0)

    def dma(eng, sem, out, in_, waits=()):
        emit(eng, (lambda e, o=out, i=in_: e.dma_start(out=o, in_=i)), waits, ((sem, 16),))
        return (sem, cnt(sem))

    def mm(out, lhsT, rhs, start, stop, waits=(), inc=False):
        emit("tensor",
             (lambda e, o=out, l=lhsT, r=rhs, st=start, sp=stop:
              e.matmul(o, l, r, start=st, stop=sp, skip_group_check=True)),
             waits, ((("pe", 1),) if inc else ()))
        return cnt("pe")

    def act(out, in_, func, bias=0.0, scale=1.0, waits=(), inc=True):
        emit("scalar",
             (lambda e, o=out, i=in_, f=func, b=bias, s=scale:
              e.activation(o, i, f, bias=b, scale=s)),
             waits, ((("act", 1),) if inc else ()))
        return cnt("act")

    def tt(out, a, b, op, waits=(), inc=False):
        emit("vector", (lambda e, o=out, x=a, y=b, p=op: e.tensor_tensor(o, x, y, p)),
             waits, ((("dve", 1),) if inc else ()))
        return cnt("dve")

    def ts(out, in_, s1, s2, op0, op1=None, waits=(), inc=False):
        if s2 is None and op1 is None:
            emit("vector", (lambda e, o=out, i=in_, a=s1, p=op0:
                            e.tensor_single_scalar(o, i, a, p)),
                 waits, ((("dve", 1),) if inc else ()))
        else:
            emit("vector",
                 (lambda e, o=out, i=in_, a=s1, b=s2, p0=op0, p1=op1:
                  e.tensor_scalar(o, i, a, b, p0, p1)),
                 waits, ((("dve", 1),) if inc else ()))
        return cnt("dve")

    def stt(out, in0, scalar, in1, op0, op1, waits=(), inc=False):
        emit("vector",
             (lambda e, o=out, i0=in0, s=scalar, i1=in1, p0=op0, p1=op1:
              e.scalar_tensor_tensor(o, i0, s, i1, p0, p1)),
             waits, ((("dve", 1),) if inc else ()))
        return cnt("dve")

    def cp(out, in_, waits=(), inc=False):
        emit("vector", (lambda e, o=out, i=in_: e.tensor_copy(o, i)),
             waits, ((("dve", 1),) if inc else ()))
        return cnt("dve")

    def ptt(out, a, b, op, waits=(), inc=False):
        emit("gpsimd", (lambda e, o=out, x=a, y=b, p=op: e.tensor_tensor(o, x, y, p)),
             waits, ((("pool", 1),) if inc else ()))
        return cnt("pool")

    def vrecip(out, in_, waits=(), inc=True):
        emit("vector", (lambda e, o=out, i=in_: e.reciprocal(o, i)),
             waits, ((("dve", 1),) if inc else ()))
        return cnt("dve")

    def vred(out, in_, op, waits=(), inc=False):
        emit("vector", (lambda e, o=out, i=in_, p=op: e.tensor_reduce(o, i, AX.X, p)),
             waits, ((("dve", 1),) if inc else ()))
        return cnt("dve")

    def memset(ap_, val):
        emit("vector", (lambda e, a=ap_, v=val: e.memset(a, v)), (), ())

    def iota(out, pattern, base, chan, waits=()):
        emit("gpsimd",
             (lambda e, o=out, p=pattern, b=base, c=chan:
              e.iota(o, p, base=b, channel_multiplier=c,
                     allow_small_or_imprecise_dtypes=True)),
             waits, ((("gi", 1),)))
        return cnt("gi")

    def wait_only(eng, waits):
        emit(eng, None, waits, ())

    with ExitStack() as st:
        E = st.enter_context
        xhat = E(nc.sbuf_tensor("xhat", [128, 32768], BF16))
        xcast = E(nc.sbuf_tensor("xcast", [128, 32768], BF16))
        ar = E(nc.sbuf_tensor("ar", [128, 20480], BF16))
        cosk = E(nc.sbuf_tensor("cosk", [128, TPC], F32))
        posr = E(nc.sbuf_tensor("posr_sb", [1, TPC], F32))
        xc2 = E(nc.sbuf_tensor("xc2", [128, 2048], F32))
        yt2 = E(nc.sbuf_tensor("yt2", [128, 2048], F32))
        sq2 = E(nc.sbuf_tensor("sq2", [128, 2048], BF16))
        qm2 = E(nc.sbuf_tensor("qm2", [1, 32], F32))
        km2 = E(nc.sbuf_tensor("km2", [1, 8], F32))
        tsc = E(nc.sbuf_tensor("tsc", [1, 8], F32))
        negc0 = E(nc.sbuf_tensor("negc0", [128, 8], F32))
        wcol = E(nc.sbuf_tensor("wcol", [128, 2], F32))
        cs16 = E(nc.sbuf_tensor("cs16", [16, 2], F32))
        w16 = E(nc.sbuf_tensor("w16", [16, 1], F32))
        selw = E(nc.sbuf_tensor("selw", [16, 128], F32))
        selw2 = E(nc.sbuf_tensor("selw2", [16, 128], F32))
        iwork = E(nc.sbuf_tensor("iwork", [128, 128], F32))
        iotac = E(nc.sbuf_tensor("iotac", [128, 1], F32))
        iotam = E(nc.sbuf_tensor("iotam", [128, 128], F32))
        idb = E(nc.sbuf_tensor("idb", [128, 128], BF16))
        onesb = E(nc.sbuf_tensor("onesb", [128, 1], BF16))
        onesf = E(nc.sbuf_tensor("onesf", [128, 1], F32))
        ones1f = E(nc.sbuf_tensor("ones1f", [1, 128], F32))
        ones1b = E(nc.sbuf_tensor("ones1b", [1, 512], BF16))
        brow = E(nc.sbuf_tensor("brow", [1, 256], BF16))
        invf = E(nc.sbuf_tensor("invf", [1, 64], F32))
        pacc = E(nc.sbuf_tensor("pacc", [128, 512], F32))
        recipb = E(nc.sbuf_tensor("recipb", [128, 512], F32))

        PSp0 = E(nc.psum_tensor("psp0", [128, 1024], F32))
        PSp1 = E(nc.psum_tensor("psp1", [128, 1024], F32))
        PSp2 = E(nc.psum_tensor("psp2", [128, 1024], F32))
        PSc = E(nc.psum_tensor("psc", [128, 512], F32))
        PSr = E(nc.psum_tensor("psr", [128, 512], F32))
        YP = (PSp0, PSp1)

        sems = {}
        for s in ("gx", "gi", "gw0", "gw1", "gb0", "gb1", "sx0", "sx1", "skv",
                  "sq", "sc", "so0", "so1", "cc", "pe", "act", "dve", "pool"):
            sems[s] = E(nc.semaphore(s))

        M = {}

        def xh(oc, nt=None):
            if nt is None:
                return xhat[:, oc * 1024:(oc + 1) * 1024]
            return xhat[:, oc * 1024 + nt * 512: oc * 1024 + (nt + 1) * 512]

        def xcs(oc, nt=None):
            if nt is None:
                return xcast[:, oc * 1024:(oc + 1) * 1024]
            return xcast[:, oc * 1024 + nt * 512: oc * 1024 + (nt + 1) * 512]

        def arc(c0, c1, r0=0, r1=128):
            return ar[r0:r1, c0:c1]

        def ck(ph):
            if not on(ph):
                raise _StopEmit()
        try:
            dma("sync", "sc", posr[:, :], posD[0:1, :])
            dma("sync", "sc", cs16[:, 0:1], bass.AP(cfD, 0, [[1, 16], [1, 1]]))
            sc_all = dma("sync", "sc", cs16[:, 1:2],
                         bass.AP(scD, 0, [[1, 16], [1, 1]]))

            gi1 = iota(iwork[:, 0:128], [[1, 128]], 0, -1)
            gi2 = iota(iotac[:, 0:1], [[0, 1]], 0, 1)
            gi3 = iota(invf[0:1, 0:64], [[1, 64]], 0, 0)
            gi4 = iota(iotam[:, :], [[0, 128]], 0, 1)

            memset(qm2[0:1, :], 0.0)
            memset(km2[0:1, :], 0.0)
            memset(tsc[0:1, :], 0.0)
            memset(onesb[:, :], 1.0)
            memset(onesf[:, :], 1.0)
            memset(ones1f[:, :], 1.0)
            memset(ones1b[:, :], 1.0)
            m_id = ts(idb[:, :], iwork[:, 0:128], 0.0, None, ALU.is_equal,
                      waits=[("gi", gi1)], inc=True)

            m_invf = act(invf[0:1, :], invf[0:1, :], AF.Exp, bias=0.0,
                         scale=-math.log(10000.0) / 64.0, waits=[("gi", gi3)])
            m_ang = None
            for nt in range(2):
                m_ang = mm(PSp0[0:64, nt * 512:(nt + 1) * 512], invf[0:1, :],
                           posr[0:1, nt * 512:(nt + 1) * 512], True, True,
                           waits=[("act", m_invf), sc_all], inc=True)
            cp(xc2[0:64, 0:1024], PSp0[0:64, :], waits=[("pe", m_ang)])
            ts(yt2[0:64, 0:1024], xc2[0:64, 0:1024], INV_2PI, MAGIC, ALU.mult, ALU.add)
            ts(yt2[0:64, 0:1024], yt2[0:64, 0:1024], -MAGIC, None, ALU.add)
            stt(xc2[64:128, 0:1024], yt2[0:64, 0:1024], -CW_C1, xc2[0:64, 0:1024],
                ALU.mult, ALU.add)
            cp(yt2[64:128, 0:1024], xc2[64:128, 0:1024])
            ts(xc2[64:128, 0:1024], yt2[0:64, 0:1024], CW_C2, None, ALU.mult)
            tt(yt2[64:128, 0:1024], yt2[64:128, 0:1024], xc2[64:128, 0:1024],
               ALU.subtract)
            ts(yt2[64:128, 0:1024], yt2[64:128, 0:1024], math.pi, None, ALU.min)
            m_r = ts(yt2[64:128, 0:1024], yt2[64:128, 0:1024], -math.pi, None,
                     ALU.max, inc=True)
            m_sin = act(cosk[64:128, :], yt2[64:128, 0:1024], AF.Sin,
                        waits=[("dve", m_r)])
            m_abs = act(xc2[64:128, 0:1024], yt2[64:128, 0:1024], AF.Abs,
                        waits=[("dve", m_r)])
            m_t = ts(xc2[64:128, 0:1024], xc2[64:128, 0:1024], -1.0, HALF_PI,
                     ALU.mult, ALU.add, waits=[("act", m_abs)], inc=True)
            m_cos = act(cosk[0:64, :], xc2[64:128, 0:1024], AF.Sin, waits=[("dve", m_t)])

            m_w16 = tt(w16[:, 0:1], cs16[:, 0:1], cs16[:, 1:2], ALU.mult,
                       waits=[sc_all], inc=True)
            m_wcol = None
            for erc in range(2):
                gie = iota(iwork[0:16, 0:128], [[1, 128]], 128 * erc, -16,
                           waits=[("dve", m_id if erc == 0 else cnt("dve"))])
                ts(selw[:, :], iwork[0:16, 0:128], 0.0, None, ALU.is_ge,
                   waits=[("gi", gie)])
                ts(selw2[:, :], iwork[0:16, 0:128], 16.0, None, ALU.is_lt)
                m_sel = tt(selw[:, :], selw[:, :], selw2[:, :], ALU.mult, inc=True)
                m_selmm = mm(PSc[:, 0:1], selw[:, :], w16[:, 0:1], True, True,
                             waits=[("dve", max(m_sel, m_w16))], inc=True)
                m_wcol = cp(wcol[:, erc:erc + 1], PSc[:, 0:1],
                            waits=[("pe", m_selmm)], inc=True)

            def ln_finalize_and_bcast(pe_stats_ms):
                ts(xc2[0:1, 1024:1536], PSp2[0:1, 0:512], 1.0 / LN_N, None, ALU.mult,
                   waits=[("pe", pe_stats_ms)])
                ts(xc2[0:1, 1536:2048], PSp2[0:1, 512:1024], 1.0 / LN_N, None, ALU.mult)
                tt(xc2[0:1, 0:1024], xc2[0:1, 1024:2048], xc2[0:1, 1024:2048], ALU.mult)
                stt(recipb[0:1, :], PSr[0:1, :], 1.0 / LN_N, xc2[0:1, 0:512],
                    ALU.mult, ALU.subtract)
                stt(pacc[0:1, :], PSc[0:1, :], 1.0 / LN_N,
                    xc2[0:1, 512:1024], ALU.mult, ALU.subtract)
                ts(recipb[0:1, :], recipb[0:1, :], EPS, None, ALU.add)
                m_v = ts(pacc[0:1, :], pacc[0:1, :], EPS, None, ALU.add, inc=True)
                act(recipb[0:1, :], recipb[0:1, :], AF.Sqrt, bias=0.0, scale=1.0,
                    waits=[("dve", m_v)])
                m_sq = act(pacc[0:1, :], pacc[0:1, :], AF.Sqrt, bias=0.0, scale=1.0)
                vrecip(xc2[0:1, 0:512], recipb[0:1, :], waits=[("act", m_sq)])
                m_rr = vrecip(xc2[0:1, 512:1024], pacc[0:1, :])
                mm(PSp2[:, 0:512], ones1f[0:1, :], xc2[0:1, 1024:1536], True, True,
                   waits=[("dve", m_rr)])
                mm(PSp2[:, 512:1024], ones1f[0:1, :], xc2[0:1, 1536:2048], True, True)
                mm(PSc[:, :], ones1f[0:1, :], xc2[0:1, 0:512], True, True)
                b4 = mm(PSr[:, :], ones1f[0:1, :], xc2[0:1, 512:1024],
                        True, True, inc=True)
                return b4

            def ln_normalize(src_fn, dst_fn, b4):
                m_norm = None
                for oc in range(NCH):
                    w = [("pe", b4)] if oc == 0 else ()
                    tt(yt2[:, 0:512], src_fn(oc, 0), PSp2[:, 0:512], ALU.subtract, waits=w)
                    tt(yt2[:, 512:1024], src_fn(oc, 1), PSp2[:, 512:1024], ALU.subtract)
                    tt(dst_fn(oc, 0), yt2[:, 0:512], PSc[:, :], ALU.mult)
                    m_norm = tt(dst_fn(oc, 1), yt2[:, 512:1024], PSr[:, :],
                                ALU.mult, inc=True)
                return m_norm

            ck("ln1")
            for oc in range(NCH):
                gx_all = dma("gpsimd", "gx", xcs(oc),
                             xin[oc * 128:(oc + 1) * 128, :])
            m_stats = None
            for oc in range(NCH):
                sl = oc % 2
                w_sq = []
                if oc == 0:
                    w_sq = [gx_all]
                elif oc >= 2:
                    w_sq = [("pe", M[("ln1s", oc - 2)])]
                m_sqr = act(sq2[:, sl * 1024:(sl + 1) * 1024], xcs(oc), AF.Square,
                            waits=w_sq)
                first = (oc == 0)
                last = (oc == NCH - 1)
                mm(PSp2[0:1, 0:512], onesb[:, :], xcs(oc, 0), first, last,
                   waits=[gx_all] if first else ())
                mm(PSp2[0:1, 512:1024], onesb[:, :], xcs(oc, 1), first, last)
                mm(PSr[0:1, :], onesb[:, :], sq2[:, sl * 1024: sl * 1024 + 512],
                   first, last, waits=[("act", m_sqr)])
                m_stats = mm(PSc[0:1, :], onesb[:, :],
                             sq2[:, sl * 1024 + 512:(sl + 1) * 1024], first, last,
                             inc=True)
                M[("ln1s", oc)] = m_stats
            m_bc1 = ln_finalize_and_bcast(m_stats)
            m_ln1 = ln_normalize(xcs, xh, m_bc1)

            def emit_prep_loads(first_wait_pe):
                for kc in range(NCH):
                    dma("gpsimd", "gw0", arc(AT0 + kc * 256, AT0 + (kc + 1) * 256),
                        atm[kc * 128:(kc + 1) * 128, :],
                        waits=[("pe", first_wait_pe)] if kc == 0 and first_wait_pe else ())
                gw = None
                for erc in range(2):
                    gw = dma("gpsimd", "gw0",
                             arc(BST0 + erc * 4096, BST0 + (erc + 1) * 4096),
                             bstm[erc * 128:(erc + 1) * 128, :])
                return gw

            def emit_xa(src_fn, gw_prep, dep_dve):
                m_xa = None
                for erc in range(2):
                    mms = [None, None]
                    for nt in range(2):
                        ps = PSc if nt == 0 else PSr
                        for kc in range(NCH):
                            w = []
                            if erc == 0 and nt == 0 and kc == 0:
                                w = [gw_prep, ("dve", dep_dve)]
                            if erc == 1 and kc == 0:
                                w = [("dve", m_xa)]
                            mms[nt] = mm(ps[:, :],
                                         arc(AT0 + kc * 256 + erc * 128,
                                             AT0 + kc * 256 + (erc + 1) * 128),
                                         src_fn(kc, nt), kc == 0, kc == NCH - 1,
                                         waits=w, inc=(kc == NCH - 1))
                    for nt in range(2):
                        ps = PSc if nt == 0 else PSr
                        m_xa = ts(arc(XA0 + erc * 1024 + nt * 512,
                                      XA0 + erc * 1024 + (nt + 1) * 512),
                                  ps[:, :], wcol[:, erc:erc + 1], None, ALU.mult,
                                  waits=[("pe", mms[nt]), ("dve", m_wcol)], inc=True)
                return m_xa

            def proj_loop(tag, n_oc, wdram, wcols, bdram, rhs_fn, lowrank, scale_half,
                          res_in, res_out, res_wait, cast_fn, fold_stats, final_out,
                          rope, to_stage, first_waits):
                M[(tag, "bankfree", 0)] = [("dve", cnt("dve"))]
                M[(tag, "bankfree", 1)] = [("dve", cnt("dve"))]
                pe_now = cnt("pe")

                def wslab_ap(p):
                    return wdram[p * 128:(p + 1) * 128, :]

                for p in range(min(2, n_oc)):
                    M[(tag, "wl", p)] = dma(
                        "gpsimd", "gw%d" % (p % 2), arc(WS0 if p % 2 == 0 else WS1,
                                            (WS0 if p % 2 == 0 else WS1) + 4096),
                        wslab_ap(p),
                        waits=(list(first_waits) + [("pe", pe_now)]) if p == 0 else ())
                    M[(tag, "bl", p)] = dma(
                        "gpsimd", "gb%d" % (p % 2),
                        brow[0:1, (p % 2) * 128:(p % 2 + 1) * 128],
                        bass.AP(bdram, p * 128, [[1, 1], [1, 128]]),
                        waits=[("pe", pe_now)] if p == 0 else ())
                    if res_in is not None:
                        M[(tag, "xl", p)] = dma(
                            "sync", "sx%d" % (p % 2),
                            xc2[:, (p % 2) * 1024:(p % 2 + 1) * 1024],
                            res_in[p * 128:(p + 1) * 128, :],
                            waits=list(res_wait) + [("pe", pe_now)])
                m_last = None
                for oc in range(n_oc):
                    par = oc % 2
                    ps = YP[par]
                    slab0 = WS0 if par == 0 else WS1
                    ymm = [None, None]
                    for nt in range(2):
                        for kc in range(NCH):
                            w = []
                            if kc == 0 and nt == 0:
                                w = [M[(tag, "wl", oc)], M[(tag, "bl", oc)]]
                                w += M[(tag, "bankfree", par)]
                            mm(ps[:, nt * 512:(nt + 1) * 512],
                               arc(slab0 + kc * 128, slab0 + (kc + 1) * 128),
                               rhs_fn(kc, nt), kc == 0, False, waits=w)
                        if lowrank:
                            for erc in range(2):
                                mm(ps[:, nt * 512:(nt + 1) * 512],
                                   arc(BST0 + erc * 4096 + oc * 128,
                                       BST0 + erc * 4096 + (oc + 1) * 128),
                                   arc(XA0 + erc * 1024 + nt * 512,
                                       XA0 + erc * 1024 + (nt + 1) * 512),
                                   False, False)
                        ymm[nt] = mm(ps[:, nt * 512:(nt + 1) * 512],
                                     brow[0:1, par * 128:(par + 1) * 128],
                                     ones1b[0:1, 0:512], False, True, inc=True)
                    M[(tag, "ymm", oc)] = ymm[1]

                    if rope is not None:
                        scale = INVSQD if rope == "q" else 1.0
                        stg = arc(QS0 + par * 1024, QS0 + (par + 1) * 1024)
                        w0 = [("pe", ymm[1])]
                        if oc >= 2:
                            w0.append(M[(tag, "st", oc - 2)])
                        stt(xc2[0:64, 0:1024], ps[0:64, :], scale, cosk[0:64, :],
                            ALU.mult, ALU.mult, waits=w0)
                        stt(yt2[0:64, 0:1024], ps[64:128, :], scale, cosk[64:128, :],
                            ALU.mult, ALU.mult)
                        tt(stg[0:64, :], xc2[0:64, 0:1024], yt2[0:64, 0:1024],
                           ALU.subtract)
                        stt(xc2[0:64, 0:1024], ps[0:64, :], scale, cosk[64:128, :],
                            ALU.mult, ALU.mult)
                        stt(yt2[0:64, 0:1024], ps[64:128, :], scale, cosk[0:64, :],
                            ALU.mult, ALU.mult)
                        m_rope = tt(stg[64:128, :], xc2[0:64, 0:1024],
                                    yt2[0:64, 0:1024], ALU.add, inc=True)
                        M[(tag, "bankfree", par)] = [("dve", m_rope)]
                        pend0 = M.setdefault((tag, "qnpend"), [])
                        if len(pend0) >= 2:
                            emit_qn(*pend0.pop(0))
                        w_sq = [("dve", m_rope)]
                        if oc >= 2:
                            w_sq.append(("pe", M[(tag, "qn", oc - 2)]))
                        m_sqr = act(sq2[:, par * 1024:(par + 1) * 1024], stg[:, :],
                                    AF.Square, waits=w_sq)

                        def emit_qn(oc2, m_sqr2, par2):
                            w0 = [("act", m_sqr2)]
                            if M.get((tag, "nrmlast")):
                                w0.append(("dve", M[(tag, "nrmlast")]))
                            mm(PSp2[0:1, 0:512], onesb[:, :],
                               sq2[:, par2 * 1024:par2 * 1024 + 512], True, True,
                               waits=w0)
                            n2 = mm(PSp2[0:1, 512:1024], onesb[:, :],
                                    sq2[:, par2 * 1024 + 512:(par2 + 1) * 1024],
                                    True, True, inc=True)
                            M[(tag, "qn", oc2)] = n2
                            dstq = qm2 if rope == "q" else km2
                            vred(dstq[0:1, oc2:oc2 + 1], PSp2[0:1, 0:512], ALU.max,
                                 waits=[("pe", n2)])
                            memset(tsc[0:1, 0:1], 0.0)
                            vred(tsc[0:1, 0:1], PSp2[0:1, 512:1024], ALU.max)
                            m_nrm = tt(dstq[0:1, oc2:oc2 + 1],
                                       dstq[0:1, oc2:oc2 + 1],
                                       tsc[0:1, 0:1], ALU.max, inc=True)
                            M[(tag, "nrm", oc2)] = m_nrm
                            M[(tag, "nrmlast")] = m_nrm
                            return m_nrm

                        pend = M[(tag, "qnpend")]
                        pend.append((oc, m_sqr, par))
                        if oc == n_oc - 1:
                            while pend:
                                m_last = emit_qn(*pend.pop(0))
                        M[(tag, "st", oc)] = to_stage(oc, stg, m_rope)
                    elif to_stage is not None:
                        stg = arc(QS0 + par * 1024, QS0 + (par + 1) * 1024)
                        w0 = [("pe", ymm[1])]
                        if oc >= 2:
                            w0.append(("pe", M[(tag, "tms", oc - 2)]))
                        cp(stg[:, 0:512], ps[:, 0:512], waits=w0)
                        m_ev = cp(stg[:, 512:1024], ps[:, 512:1024], inc=True)
                        M[(tag, "bankfree", par)] = [("dve", m_ev)]
                        tms = None
                        for tti in range(8):
                            psd = PSc if tti < 4 else PSr
                            w = []
                            if tti == 0:
                                w = [("dve", m_ev)]
                                if oc >= 1:
                                    w.append(("dve", M[(tag, "vtm", oc - 1)]))
                            tms = mm(psd[:, (tti % 4) * 128:(tti % 4 + 1) * 128],
                                     stg[:, tti * 128:(tti + 1) * 128], idb[:, :],
                                     True, True, waits=w, inc=(tti == 7))
                        M[(tag, "tms", oc)] = tms
                        vst = arc(VT0 + par * 1024, VT0 + (par + 1) * 1024)
                        w = [("pe", tms)]
                        if oc >= 2:
                            w.append(M[(tag, "st", oc - 2)])
                        cp(vst[:, 0:512], PSc[:, :], waits=w)
                        m_vtm = cp(vst[:, 512:1024], PSr[:, :], inc=True)
                        M[(tag, "vtm", oc)] = m_vtm
                        M[(tag, "st", oc)] = to_stage(oc, vst, m_vtm)
                        m_last = m_vtm
                    else:
                        for nt in range(2):
                            w = [("pe", ymm[nt]), M[(tag, "xl", oc)]]
                            if oc >= 2:
                                w.append(M[(tag, "st", oc - 2)])
                            M[(tag, "stt", oc, nt)] = stt(
                                yt2[:, par * 1024 + nt * 512: par * 1024 + (nt + 1) * 512],
                                ps[:, nt * 512:(nt + 1) * 512], scale_half,
                                xc2[:, par * 1024 + nt * 512: par * 1024 + (nt + 1) * 512],
                                ALU.mult, ALU.add, waits=w, inc=True)
                        M[(tag, "bankfree", par)] = [("dve", M[(tag, "stt", oc, 1)])]
                        dst = outD if final_out else res_out
                        M[(tag, "st", oc)] = dma(
                            "sync", "so%d" % par, dst[oc * 128:(oc + 1) * 128, :],
                            yt2[:, par * 1024:(par + 1) * 1024],
                            waits=[("dve", M[(tag, "stt", oc, 1)])])
                        if fold_stats:
                            m_cast = cp(cast_fn(oc), yt2[:, par * 1024:(par + 1) * 1024],
                                        inc=True)
                            w_sq = [("dve", m_cast)]
                            if oc >= 2:
                                w_sq.append(("pe", M[(tag, "lns", oc - 2)]))
                            m_sqr = act(sq2[:, par * 1024:(par + 1) * 1024],
                                        cast_fn(oc), AF.Square, waits=w_sq)
                            first = (oc == 0)
                            last = (oc == n_oc - 1)
                            mm(PSp2[0:1, 0:512], onesb[:, :], cast_fn(oc, 0), first, last,
                               waits=[("dve", m_cast)])
                            mm(PSp2[0:1, 512:1024], onesb[:, :], cast_fn(oc, 1), first, last)
                            mm(PSr[0:1, :], onesb[:, :],
                               sq2[:, par * 1024: par * 1024 + 512], first, last,
                               waits=[("act", m_sqr)])
                            M[(tag, "lns", oc)] = mm(
                                PSc[0:1, :], onesb[:, :],
                                sq2[:, par * 1024 + 512:(par + 1) * 1024], first, last,
                                inc=True)
                            m_last = M[(tag, "lns", oc)]
                        else:
                            m_last = M[(tag, "stt", oc, 1)]
                    if oc + 2 < n_oc:
                        p = oc + 2
                        M[(tag, "wl", p)] = dma(
                            "gpsimd", "gw%d" % (p % 2), arc(WS0 if p % 2 == 0 else WS1,
                                                (WS0 if p % 2 == 0 else WS1) + 4096),
                            wslab_ap(p), waits=[("pe", ymm[1])])
                        M[(tag, "bl", p)] = dma(
                            "gpsimd", "gb%d" % (p % 2),
                            brow[0:1, (p % 2) * 128:(p % 2 + 1) * 128],
                            bass.AP(bdram, p * 128, [[1, 1], [1, 128]]),
                            waits=[("pe", ymm[1])])
                        if res_in is not None:
                            M[(tag, "xl", p)] = dma(
                                "sync", "sx%d" % (p % 2),
                                xc2[:, (p % 2) * 1024:(p % 2 + 1) * 1024],
                                res_in[p * 128:(p + 1) * 128, :],
                                waits=list(res_wait) + [("dve", M[(tag, "stt", oc, 1)])])
                return m_last

            ck("xa")
            gw_prep = emit_prep_loads(0)
            m_xa1 = emit_xa(lambda kc, nt: xh(kc, nt), gw_prep, m_ln1)
            ck("f1")
            m_f1 = proj_loop("f1", f1_noc, wft, DM, bf_, lambda kc, nt: xh(kc, nt),
                             True, 0.5, xin, xr1, (), xcs, True, False, None, None,
                             first_waits=[("dve", m_xa1)])
            ck("ln2")
            m_bc2 = ln_finalize_and_bcast(m_f1)
            m_ln2 = ln_normalize(xcs, xh, m_bc2)

            ck("k")
            def k_store(oc, stg, m_rope):
                return dma("sync", "so%d" % (oc % 2),
                           kcon[oc * 128:(oc + 1) * 128, :], stg[:, :],
                           waits=[("dve", m_rope)])

            proj_loop("k", 8, wk, 1024, bk, lambda kc, nt: xh(kc, nt),
                      False, 1.0, None, None, (), None, False, False, "k", k_store,
                      first_waits=[("dve", m_ln2)])

            so_k = [("so0", cnt("so0")), ("so1", cnt("so1"))]
            emit("gpsimd",
                 (lambda e: e.collective_compute(
                     "AllGather", ALU.bypass, replica_groups=[list(range(8))],
                     ins=[kcon.ap().opt()], outs=[kg.ap().opt()])),
                 so_k, ((("cc", 1),)))
            ck("v")
            def v_store(oc, vst, m_vtm):
                last = None
                for tti in range(8):
                    last = dma("sync", "so%d" % (oc % 2),
                               vcon[(oc * 8 + tti) * 128:(oc * 8 + tti + 1) * 128, :],
                               vst[:, tti * 128:(tti + 1) * 128],
                               waits=[("dve", m_vtm)] if tti == 0 else ())
                return last

            proj_loop("v", 8, wv, 1024, bv, lambda kc, nt: xh(kc, nt),
                      False, 1.0, None, None, (), None, False, False, None, v_store,
                      first_waits=())

            ck("cc")
            so_v = [("so0", cnt("so0")), ("so1", cnt("so1"))]
            emit("gpsimd",
                 (lambda e: e.collective_compute(
                     "AllGather", ALU.bypass, replica_groups=[list(range(8))],
                     ins=[vcon.ap().opt()], outs=[vg.ap().opt()])),
                 so_v, ((("cc", 1),)))
            cc_all = cnt("cc")

            ck("q")
            def q_store(oc, stg, m_rope):
                return dma("sync", "so%d" % (oc % 2),
                           qd[:, oc * 1024:(oc + 1) * 1024], stg[:, :],
                           waits=[("dve", m_rope)])

            proj_loop("q", 32, wq, DM, bq, lambda kc, nt: xh(kc, nt),
                      False, 1.0, None, None, (), None, False, False, "q", q_store,
                      first_waits=())

            so_all = [("so0", cnt("so0")), ("so1", cnt("so1"))]
            m_nrm_all = max(max(M[("q", "nrm", o)] for o in range(32)),
                            max(M[("k", "nrm", o)] for o in range(8)))
            m_negc = None
            for kvh in range(8):
                memset(tsc[0:1, 0:1], 0.0)
                vred(tsc[0:1, 0:1], qm2[0:1, 4 * kvh:4 * kvh + 4], ALU.max,
                     waits=[("dve", m_nrm_all)] if kvh == 0 else ())
                m_t2 = tt(tsc[0:1, 1:2], tsc[0:1, 0:1], km2[0:1, kvh:kvh + 1],
                          ALU.mult, inc=True)
                m_sq3 = act(tsc[0:1, 2:3], tsc[0:1, 1:2], AF.Sqrt,
                            waits=[("dve", m_t2)])
                m_t4 = ts(tsc[0:1, 3:4], tsc[0:1, 2:3], 1.0, -1.0, ALU.add, ALU.mult,
                          waits=[("act", m_sq3)], inc=True)
                m_bcn = mm(PSc[:, 0:1], ones1f[0:1, :], tsc[0:1, 3:4], True, True,
                           waits=[("dve", m_t4)], inc=True)
                m_negc = cp(negc0[:, kvh:kvh + 1], PSc[:, 0:1],
                            waits=[("pe", m_bcn)], inc=True)

            ck("p3")
            m_mask = None
            for g in range(NG):
                mb_ = mm(PSc[:, 0:128], ones1f[0:1, :],
                         posr[0:1, g * 128:(g + 1) * 128], True, True,
                         waits=[("dve", m_negc if g == 0 else cnt("dve"))], inc=True)
                for mp in range(8):
                    w = [("pe", mb_)] if mp == 0 else ()
                    if mp == 0:
                        w = list(w) + [("gi", gi4)]
                    ts(iwork[:, 0:128], PSc[:, 0:128],
                       -float(1024 * g + 128 * mp), None, ALU.add, waits=w)
                    tt(iwork[:, 0:128], iwork[:, 0:128], iotam[:, :], ALU.subtract)
                    m_mask = ts(arc(MK0 + (g * 8 + mp) * 128,
                                    MK0 + (g * 8 + mp + 1) * 128),
                                iwork[:, 0:128], 0.0, None, ALU.is_ge, inc=True)

            PSPS = (PSp0, PSp1, PSp2)
            NRING = 6
            pair_ctr = 0
            exp_bank = [0, 0, 0]
            ring_ctx = [0] * NRING
            ring_sum = [0] * NRING
            den_pend = []
            m_ev = 0
            ps_merge = [0, 0]
            m_ctx_evac = 0
            m_acc = 0
            def kv_loads(kvh2, waits0):
                base = (kvh2 % 2) * 16384
                dma("sync", "skv",
                    bass.AP(xhat, base, [[32768, 128], [1024, 8], [1, 1024]]),
                    bass.AP(kg, kvh2 * 128 * TPC,
                            [[TPC, 128], [1024 * TPC, 8], [1, 1024]]),
                    waits=waits0)
                r = None
                for c in range(8):
                    r = dma("sync", "skv",
                            bass.AP(xhat, base + 8192 + c * 1024,
                                    [[32768, 128], [128, 8], [1, 128]]),
                            bass.AP(vg, (c * 8 * TPC + kvh2 * TPC) * 128,
                                    [[128, 128], [128 * 128, 8], [1, 128]]))
                return r

            kv_pend = {0: kv_loads(0, [("cc", cc_all)]),
                       1: kv_loads(1, ())}
            q_pend = dma("sync", "sq", arc(QB0, QB0 + 4096), qd[:, 0:4096])
            prev_pe_done = {}
            for kvh in range(8):
                Kbase = (kvh % 2) * 16384
                Vbase = Kbase + 8192
                kv_ms = kv_pend[kvh % 2]
                q_ms = q_pend
                for g in range(NG):
                    q4 = bass.AP(ar, QB0 + g * 128, [[20480, 128], [1024, 4], [1, 128]])
                    tiles = [(c, kb) for kb in range(g + 1) for c in range(8)]
                    npair = len(tiles) // 2
                    pair_meta = []
                    ctx_ms = 0
                    ctx_state = [False]

                    def emit_ctx_pair(pj2):
                        nonlocal ctx_ms
                        slot2, m_exp2, m_msk2 = pair_meta[pj2]
                        prt = tiles[2 * pj2:2 * pj2 + 2]
                        for j, (c, kb) in enumerate(prt):
                            w = []
                            if j == 0:
                                w = [("act", m_exp2)]
                                if m_msk2:
                                    w.append(("dve", m_msk2))
                                if not ctx_state[0] and m_ctx_evac:
                                    w.append(("dve", m_ctx_evac))
                            ctx_ms = mm(PSc[:, :],
                                        bass.AP(xhat, Vbase + (c * 8 + kb) * 128,
                                                [[32768, 128], [1, 128]]),
                                        arc(PR0 + slot2 * 1024 + j * 512,
                                            PR0 + slot2 * 1024 + (j + 1) * 512),
                                        not ctx_state[0],
                                        pj2 == npair - 1 and j == 1,
                                        waits=w, inc=(j == 1))
                            ctx_state[0] = True
                        ring_ctx[slot2] = ctx_ms

                    for pj in range(npair):
                        prt = tiles[2 * pj:2 * pj + 2]
                        bank = pair_ctr % 3
                        psp = PSPS[bank]
                        slot = pair_ctr % NRING
                        sc_ms = None
                        for j, (c, kb) in enumerate(prt):
                            w = []
                            if pj == 0 and g == 0 and j == 0:
                                w = [kv_ms, q_ms]
                                if kvh == 0:
                                    w.append(("dve", m_mask))
                            if j == 0 and exp_bank[bank]:
                                w.append(("act", exp_bank[bank]))
                            sc_ms = mm(psp[:, j * 512:(j + 1) * 512],
                                       bass.AP(xhat, Kbase + (c * 8 + kb) * 128,
                                               [[32768, 128], [1, 128]]),
                                       q4, True, True, waits=w, inc=(j == 1))
                        w = [("pe", sc_ms)]
                        if ring_ctx[slot]:
                            w.append(("pe", ring_ctx[slot]))
                        if ring_sum[slot]:
                            w.append(("dve", ring_sum[slot]))
                        m_exp = act(arc(PR0 + slot * 1024, PR0 + (slot + 1) * 1024),
                                    psp[:, :], AF.Exp, bias=negc0[:, kvh:kvh + 1],
                                    scale=1.0, waits=w)
                        exp_bank[bank] = m_exp
                        m_msk = 0
                        for j, (c, kb) in enumerate(prt):
                            if kb == g:
                                mp = (c + g) % 8
                                sbase = PR0 + slot * 1024 + j * 512
                                m_msk = tt(arc(sbase, sbase + 512),
                                           arc(sbase, sbase + 512),
                                           bass.AP(ar, MK0 + (g * 8 + mp) * 128,
                                                   [[20480, 128], [0, 4], [1, 128]]),
                                           ALU.mult,
                                           waits=[("act", m_exp)] if m_msk == 0 else (),
                                           inc=True)
                        pair_meta.append((slot, m_exp, m_msk))
                        w = [("act", m_exp)]
                        m_psum = tt(arc(PSM0 + (pj % 2) * 512,
                                        PSM0 + (pj % 2 + 1) * 512),
                                    arc(PR0 + slot * 1024, PR0 + slot * 1024 + 512),
                                    arc(PR0 + slot * 1024 + 512,
                                        PR0 + (slot + 1) * 1024),
                                    ALU.add, waits=w, inc=True)
                        ring_sum[slot] = m_psum
                        if pj >= 4:
                            emit_ctx_pair(pj - 4)
                        if pj == 2 and den_pend:
                            m_ev = den_tail(*den_pend.pop(0))
                        if pj % 2 == 1:
                            tt(arc(QDB0, QDB0 + 512), arc(PSM0, PSM0 + 512),
                               arc(PSM0 + 512, PSM0 + 1024), ALU.add)
                            if pj == 1:
                                m_acc = cp(pacc[:, :], arc(QDB0, QDB0 + 512), inc=True)
                            else:
                                m_acc = tt(pacc[:, :], pacc[:, :],
                                           arc(QDB0, QDB0 + 512), ALU.add, inc=True)
                            ps_merge[0] = m_acc
                            ps_merge[1] = m_acc
                        pair_ctr += 1
                    for pj2 in range(max(0, npair - 4), npair):
                        emit_ctx_pair(pj2)
                    m_den = mm(PSr[0:1, :], onesf[:, :], pacc[:, :], True, True,
                               waits=[("dve", m_acc)], inc=True)
                    m_ctx_evac = cp(arc(19968, 20480), PSc[:, :],
                                    waits=[("pe", ctx_ms)], inc=True)
                    ts(xc2[0:1, 0:512], PSr[0:1, :], 1e-38, None, ALU.add,
                       waits=[("pe", m_den)])
                    m_rec = vrecip(xc2[0:1, 512:1024], xc2[0:1, 0:512])

                    def den_tail(kvh2, g2, m_rec2):
                        m_rbc = mm(PSr[:, :], ones1f[0:1, :], xc2[0:1, 512:1024],
                                   True, True, waits=[("dve", m_rec2)], inc=True)
                        m_rcp = cp(recipb[:, :], PSr[:, :], waits=[("pe", m_rbc)],
                                   inc=True)
                        m_ev2 = None
                        for h in range(4):
                            col = (4 * kvh2 + h) * 1024 + g2 * 128
                            m_ev2 = tt(xcast[:, col:col + 128],
                                       arc(19968 + h * 128, 19968 + (h + 1) * 128),
                                       recipb[:, h * 128:(h + 1) * 128], ALU.mult,
                                       inc=(h == 3))
                        return m_ev2
                    den_pend.append((kvh, g, m_rec))
                prev_pe_done[kvh] = cnt("pe")
                if kvh + 1 < 8:
                    q_pend = dma("sync", "sq", arc(QB0, QB0 + 4096),
                                 qd[:, (kvh + 1) * 4096:(kvh + 2) * 4096],
                                 waits=[("pe", prev_pe_done[kvh])])
                if kvh + 2 < 8:
                    kv_pend[kvh % 2] = kv_loads(kvh + 2,
                                                [("pe", prev_pe_done[kvh])])
            while den_pend:
                m_ev = den_tail(*den_pend.pop(0))
            m_attn = m_ev

            ck("p4")
            m_p4 = proj_loop("o", NCH, wo, DM, bo_, lambda kc, nt: xcs(kc, nt),
                             False, 1.0, xr1, xr2, so_all, xh, True, False,
                             None, None, first_waits=[("dve", m_attn)])
            ck("ln3")
            m_bc3 = ln_finalize_and_bcast(m_p4)
            m_ln3 = ln_normalize(xh, xcs, m_bc3)
            so_p4 = [("so0", cnt("so0")), ("so1", cnt("so1"))]

            ck("f2")
            gw_prep2 = emit_prep_loads(cnt("pe"))
            m_xa2 = emit_xa(lambda kc, nt: xcs(kc, nt), gw_prep2, m_ln3)
            proj_loop("f2", NCH, wft, DM, bf_, lambda kc, nt: xcs(kc, nt),
                      True, 0.5, xr2, None, so_p4, None, False, True,
                      None, None, first_waits=[("dve", m_xa2)])

            wait_only("sync", [("so0", cnt("so0")), ("so1", cnt("so1"))])
        except _StopEmit:
            allsems = [(sname, cnt(sname)) for sname in
                       ("pe", "act", "dve", "cc", "gx", "gi", "gw0", "gw1",
                        "gb0", "gb1", "sx0", "sx1", "skv", "sq", "sc",
                        "so0", "so1") if cnt(sname) > 0]
            wait_only("sync", allsems)
            wait_only("gpsimd", allsems)
            if stop_at >= _PHASES.index("p4"):
                dma("sync", "so0", outD[:, :], xr2[:, :])
            elif stop_at >= _PHASES.index("p3"):
                cp(xcast[:, 8:16], negc0[:, :])
                cp(xcast[0:1, 16:48], qm2[0:1, :])
                cp(xcast[0:1, 48:56], km2[0:1, :])
                cp(xcast[0:1, 56:64], tsc[0:1, :], inc=True)
                wait_only("gpsimd", [("dve", cnt("dve"))])
                dma("gpsimd", "so0",
                    bass.AP(outD, 0, [[32768, 128], [1, 32768]]),
                    xcast[:, :])

            elif stop_at >= _PHASES.index("cc"):
                dma("gpsimd", "so0",
                    bass.AP(outD, 0, [[1024, 4096], [1, 1024]]),
                    kg[0:4096, :])
            elif stop_at >= _PHASES.index("q"):
                dma("gpsimd", "so0",
                    bass.AP(outD, 0, [[32768, 128], [1, 32768]]),
                    bass.AP(qd, 0, [[32768, 128], [1, 32768]]))
            elif stop_at >= _PHASES.index("f1"):
                dma("sync", "so0", outD[:, :], xr1[:, :])
            else:
                for oc in range(NCH):
                    dma("sync", "so0", outD[oc * 128:(oc + 1) * 128, :],
                        yt2[:, 0:1024])
            m_dbg = mm(PSc[:, :], ones1f[0:1, :], posr[0:1, 0:512], True, True,
                       inc=True)
            cp(yt2[:, 0:512], PSc[:, :], waits=[("pe", m_dbg)])
            cp(yt2[:, 512:520], negc0[:, :], inc=True)
            wait_only("sync", [("dve", cnt("dve"))])
            dma("sync", "so0", outD[3968:4096, :], yt2[:, 0:1024])
            wait_only("sync", [("so0", cnt("so0")), ("so1", cnt("so1"))])


        block = E(nc.Block())

        def replay(name):
            def f(e):
                for make, waits, incs in plan[name]:
                    for (s, v) in waits:
                        e.wait_ge(sems[s], v)
                    if make is None:
                        continue
                    inst = make(e)
                    for (s, a) in incs:
                        inst = inst.then_inc(sems[s], a)
            return f

        _CACHE["C"] = dict(C)
        block.sync(replay("sync"))
        block.scalar(replay("scalar"))
        block.vector(replay("vector"))
        block.tensor(replay("tensor"))
        block.gpsimd(replay("gpsimd"))

    return nc


def kernel(**inputs):
    if "nc" not in _CACHE:
        _CACHE["nc"] = build()
    nc = _CACHE["nc"]

    f32 = lambda k: np.ascontiguousarray(np.asarray(inputs[k], np.float32))
    x = f32("L0_attn_hidden_states")

    def blockw(w):
        dmo = w.shape[1]
        t = w.reshape(32, 128, dmo // 128, 128)
        return np.ascontiguousarray(t.transpose(2, 1, 0, 3).reshape(dmo, 4096))

    wft = blockw(np.ascontiguousarray(f32("L0_ffn_W").T))
    atm = np.ascontiguousarray(f32("L0_ffn_A").reshape(256, 4096).T)
    bstm = np.ascontiguousarray(f32("L0_ffn_Bf").transpose(0, 2, 1).reshape(256, 4096))
    shared = {
        "wq": blockw(f32("L0_attn_Wq")), "wk": blockw(f32("L0_attn_Wk")),
        "wv": blockw(f32("L0_attn_Wv")),
        "wo": blockw(f32("L0_attn_Wo")), "wft": wft,
        "bq": f32("L0_attn_bq"), "bk": f32("L0_attn_bk"), "bv": f32("L0_attn_bv"),
        "bo": f32("L0_attn_bo"), "bf": f32("L0_ffn_bias"),
        "atm": atm, "bstm": bstm,
        "cf": f32("L0_ffn_coeffs"), "sc": f32("L0_ffn_scales"),
    }
    in_maps = []
    rows_per_core = []
    for i in range(8):
        tiles = [8 * g + (g + i) % 8 for g in range(8)]
        rows = np.concatenate([np.arange(t * 128, (t + 1) * 128) for t in tiles])
        rows_per_core.append(rows)
        in_maps.append(dict(shared,
                            xin=np.ascontiguousarray(x[0, rows, :].T),
                            posr=np.ascontiguousarray(
                                rows.astype(np.float32).reshape(1, 1024))))

    res = run_bass_kernel_spmd(nc, in_maps, core_ids=list(range(8)))

    out = np.empty((1, 8192, 4096), np.float32)
    for i in range(8):
        out[0, rows_per_core[i], :] = res.results[i]["out"].T
    return out
